# version 06
# baseline (speedup 1.0000x reference)
import numpy as np
from contextlib import ExitStack
import concourse.bass as bass
import concourse.mybir as mybir
from concourse.bass_utils import run_bass_kernel_spmd
from concourse.alu_op_type import AluOpType

F32 = mybir.dt.float32
BF16 = mybir.dt.bfloat16
AF = mybir.ActivationFunctionType
AX = mybir.AxisListType.X

B, S, H = 4, 2048, 4096
NH = 32
QL, KVL = 1536, 512
DN, DR = 128, 64
QH = DN + DR
VD = 128
OWN = 1024
INV_SQRT_QH = float(1.0 / np.sqrt(QH))

A_PASSES = 8
A_CH = 4
CT = 256
NCT = OWN // CT


class Prog:

    def __init__(self):
        self.ops = []
        self.cnt = {}

    def emit(self, eng, fn, waits=(), inc=None, amt=1):
        ws = [w for w in waits if w is not None]
        incs = []
        if inc is not None:
            self.cnt[inc] = self.cnt.get(inc, 0) + amt
            incs = [(inc, amt)]
        self.ops.append((eng, fn, ws, incs))
        return (inc, self.cnt[inc]) if inc is not None else None

    def replay(self, eng_name, raw, sems):
        water = {}
        for eng, fn, waits, incs in self.ops:
            if eng != eng_name:
                continue
            for s, v in waits:
                if water.get(s, 0) < v:
                    raw.wait_ge(sems[s], v)
                    water[s] = v
            inst = fn(raw)
            for s, amt in incs:
                inst.then_inc(sems[s], amt)


def build_nc():
    nc = bass.Bass("TRN2", target_bir_lowering=False)
    d = {}
    d["xTfull"] = nc.dram_tensor("xTfull", [H, S], F32, kind="ExternalInput").ap()
    d["xTown"] = nc.dram_tensor("xTown", [H, OWN], F32, kind="ExternalInput").ap()
    d["wqcT"] = nc.dram_tensor("wqcT", [H, QL], F32, kind="ExternalInput").ap()
    d["wqupT"] = nc.dram_tensor("wqupT", [QL, NH * QH], F32, kind="ExternalInput").ap()
    d["wkvcT"] = nc.dram_tensor("wkvcT", [H, 640], F32, kind="ExternalInput").ap()
    d["wkvupT"] = nc.dram_tensor("wkvupT", [KVL, NH * (DN + VD)], F32, kind="ExternalInput").ap()
    d["woT"] = nc.dram_tensor("woT", [VD, H], F32, kind="ExternalInput").ap()
    d["wffnT"] = nc.dram_tensor("wffnT", [H, H], F32, kind="ExternalInput").ap()
    d["waT"] = nc.dram_tensor("waT", [H, 64], F32, kind="ExternalInput").ap()
    d["wbT"] = nc.dram_tensor("wbT", [64, H], F32, kind="ExternalInput").ap()
    d["wgP"] = nc.dram_tensor("wgP", [H, 128], F32, kind="ExternalInput").ap()
    d["bffn"] = nc.dram_tensor("bffn", [128, 32], F32, kind="ExternalInput").ap()
    d["gateb"] = nc.dram_tensor("gateb", [1, 4], F32, kind="ExternalInput").ap()
    d["ident"] = nc.dram_tensor("ident", [128, 128], F32, kind="ExternalInput").ap()
    d["ones1"] = nc.dram_tensor("ones1", [1, 128], F32, kind="ExternalInput").ap()
    d["onesc"] = nc.dram_tensor("onesc", [128, 1], F32, kind="ExternalInput").ap()
    d["onesr"] = nc.dram_tensor("onesr", [1, 512], F32, kind="ExternalInput").ap()
    d["segp"] = nc.dram_tensor("segp", [1, 512], F32, kind="ExternalInput").ap()
    yT = nc.dram_tensor("yT", [H, OWN], F32, kind="ExternalOutput").ap()
    wfh_d = nc.dram_tensor("wfh_d", [H, H], BF16).ap()
    wfl_d = nc.dram_tensor("wfl_d", [H, H], BF16).ap()

    P = Prog()

    with ExitStack() as ctx:
        E = ctx.enter_context
        oacc = E(nc.sbuf_tensor("oacc", [128, OWN], F32))
        bfs = E(nc.sbuf_tensor("bfs", [128, 32], F32))
        gbs = E(nc.sbuf_tensor("gbs", [1, 4], F32))
        idt = E(nc.sbuf_tensor("idt", [128, 128], F32))
        on1 = E(nc.sbuf_tensor("on1", [1, 128], F32))
        onc = E(nc.sbuf_tensor("onc", [128, 1], F32))
        onr = E(nc.sbuf_tensor("onr", [1, 512], F32))
        sgp = E(nc.sbuf_tensor("sgp", [1, 512], F32))
        fstg = E(nc.sbuf_tensor("fstg", [128, 512], F32))
        fsh = E(nc.sbuf_tensor("fsh", [128, 512], BF16))
        fsl = E(nc.sbuf_tensor("fsl", [128, 512], BF16))
        pbig = E(nc.psum_tensor("pbig", [128, 2048], F32))
        pA = E(nc.psum_tensor("pA", [128, 512], F32))
        pB = E(nc.psum_tensor("pB", [128, 512], F32))
        pC = E(nc.psum_tensor("pC", [128, 512], F32))
        pD = E(nc.psum_tensor("pD", [128, 512], F32))
        sems = {s: E(nc.semaphore(s)) for s in ("ld0", "ldc", "lxsl", "lxso", "lwqc", "lwkvc", "lqup", "lkvup", "lxo0", "lxo1", "lwf0", "lwf1", "wfo", "wfi", "out", "te", "ve", "se")}
        block = E(nc.Block())
        ctxAB = ctx.enter_context(ExitStack())
        EAB = ctxAB.enter_context
        qc = EAB(nc.sbuf_tensor("qc", [128, 2048 + 12 * OWN], F32))
        kvc = EAB(nc.sbuf_tensor("kvc", [128, 2048 + 5 * S], F32))
        ctxA = ExitStack()
        EA = ctxA.enter_context
        xsl = [EA(nc.sbuf_tensor("xsl0", [128, A_CH * S], F32))]
        xso = [EA(nc.sbuf_tensor("xso0", [128, A_CH * OWN], F32))]
        wqcs = [EA(nc.sbuf_tensor("wqcs0", [128, A_CH * QL], F32))]
        wkvcs = [EA(nc.sbuf_tensor("wkvcs0", [128, A_CH * 640], F32))]

        em = P.emit

        def dma(dst, src):
            return lambda raw: raw.dma_start(out=dst, in_=src)

        def dma3(dst_t, src_ap, nch, w):
            dst = dst_t[:].rearrange("p (c t) -> p c t", c=nch)
            src = src_ap.rearrange("(c p) t -> p c t", p=128)
            return lambda raw: raw.dma_start(out=dst, in_=src)

        def dma3_into(dst_t, col0, src_ap, nch, w):
            dst = dst_t[:, col0:col0 + nch * w].rearrange("p (c t) -> p c t", c=nch)
            src = src_ap.rearrange("(c p) t -> p c t", p=128)
            return lambda raw: raw.dma_start(out=dst, in_=src)

        t_const = None
        for dst, src in ((idt[:], d["ident"]), (on1[:], d["ones1"]), (onc[:], d["onesc"]),
                         (onr[:], d["onesr"]), (sgp[:], d["segp"]), (bfs[:], d["bffn"]),
                         (gbs[:], d["gateb"])):
            t_const = em("sy", dma(dst, src), inc="ld0", amt=16)

        pbanks = [pbig[:, i * 512:(i + 1) * 512] for i in range(4)]
        bank_free = [None, None, None, None]
        bank_i = [0]

        def next_bank():
            i = bank_i[0] % 4
            bank_i[0] += 1
            return i

        xsl_free = [None, None]
        xso_free = [None, None]
        wqc_free = [None, None]
        wkvc_free = [None, None]
        tok_xsl = [None, None]
        tok_xso = [None, None]
        tok_wqc = [None, None]
        tok_wkvc = [None, None]

        def phaseA_loads(p):
            bi = 0
            r0 = p * A_CH * 128
            tok_xsl[bi] = em("sy", dma3(xsl[bi], d["xTfull"][r0:r0 + A_CH * 128, :], A_CH, S),
                             waits=[xsl_free[bi]], inc="lxsl", amt=16)
            tok_xso[bi] = em("sy", dma3(xso[bi], d["xTown"][r0:r0 + A_CH * 128, :], A_CH, OWN),
                             waits=[xso_free[bi]], inc="lxso", amt=16)
            tok_wqc[bi] = em("sy", dma3(wqcs[bi], d["wqcT"][r0:r0 + A_CH * 128, :], A_CH, QL),
                             waits=[wqc_free[bi]], inc="lwqc", amt=16)
            tok_wkvc[bi] = em("sy", dma3(wkvcs[bi], d["wkvcT"][r0:r0 + A_CH * 128, :], A_CH, 640),
                              waits=[wkvc_free[bi]], inc="lwkvc", amt=16)

        phaseA_loads(0)
        for p in range(A_PASSES):
            bi = 0
            for m in range(12):
                for n in range(2):
                    bk = next_bank()
                    ws = [tok_wqc[bi], tok_xso[bi], bank_free[bk]]
                    tk = None
                    for k in range(A_CH):
                        def mk(mm=m, nn=n, kk=k, b=bk, bb=bi):
                            def f(raw):
                                return raw.matmul(pbanks[b][:],
                                                  wqcs[bb][:, kk * QL + mm * 128: kk * QL + (mm + 1) * 128],
                                                  xso[bb][:, kk * OWN + nn * 512: kk * OWN + (nn + 1) * 512],
                                                  start=(kk == 0), stop=(kk == A_CH - 1))
                            return f
                        tk = em("te", mk(), waits=ws if k == 0 else (),
                                inc="te" if k == A_CH - 1 else None)
                    dstc = qc[:, 2048 + m * OWN + n * 512: 2048 + m * OWN + (n + 1) * 512]
                    if p == 0:
                        fn = (lambda ds, b: lambda raw: raw.tensor_copy(ds, pbanks[b][:]))(dstc, bk)
                    else:
                        fn = (lambda ds, b: lambda raw: raw.tensor_tensor(ds, pbanks[b][:], ds, AluOpType.add))(dstc, bk)
                    bank_free[bk] = em("ve", fn, waits=[tk], inc="ve")
            for m in range(5):
                mw = 128
                for n in range(4):
                    bk = next_bank()
                    ws = [tok_wkvc[bi], tok_xsl[bi], bank_free[bk]]
                    tk = None
                    for k in range(A_CH):
                        def mk(mm=m, nn=n, kk=k, b=bk, bb=bi, w=mw):
                            def f(raw):
                                return raw.matmul(pbanks[b][0:w, :],
                                                  wkvcs[bb][:, kk * 640 + mm * 128: kk * 640 + mm * 128 + w],
                                                  xsl[bb][:, kk * S + nn * 512: kk * S + (nn + 1) * 512],
                                                  start=(kk == 0), stop=(kk == A_CH - 1))
                            return f
                        tk = em("te", mk(), waits=ws if k == 0 else (),
                                inc="te" if k == A_CH - 1 else None)
                    dstc = kvc[0:mw, 2048 + m * S + n * 512: 2048 + m * S + (n + 1) * 512]
                    if p == 0:
                        fn = (lambda ds, b, w=mw: lambda raw: raw.tensor_copy(ds, pbanks[b][0:w, :]))(dstc, bk)
                    else:
                        fn = (lambda ds, b, w=mw: lambda raw: raw.tensor_tensor(ds, pbanks[b][0:w, :], ds, AluOpType.add))(dstc, bk)
                    bank_free[bk] = em("ve", fn, waits=[tk], inc="ve")
            free_tok = tk
            xsl_free[bi] = free_tok
            xso_free[bi] = free_tok
            wqc_free[bi] = free_tok
            wkvc_free[bi] = free_tok
            if p + 1 < A_PASSES:
                phaseA_loads(p + 1)

        qc_ready = bank_free[(bank_i[0] - 1) % 4]
        kvc_ready = qc_ready

        fs_free = None
        for m5 in range(32):
            for h5 in range(8):
                r5 = 512 * h5
                def mfs(mm=m5, r0=r5):
                    dst = fstg[:].rearrange("p (c t) -> p c t", c=4)
                    src = d["wffnT"][r0:r0 + 512, mm * 128:(mm + 1) * 128].rearrange("(c p) t -> p c t", p=128)
                    return lambda raw: raw.dma_start(out=dst, in_=src)
                t_fs = em("gp", mfs(), waits=[fs_free], inc="wfi", amt=16)
                t_fh = em("ve", lambda raw: raw.tensor_copy(fsh[:], fstg[:]), waits=[t_fs, fs_free], inc="ve")
                t_fl = em("ve", lambda raw: raw.tensor_tensor(fsl[:], fstg[:], fsh[:], AluOpType.subtract),
                          waits=[t_fh], inc="ve")
                def mfo(mm=m5, r0=r5, which=0):
                    src_t = fsh if which == 0 else fsl
                    dst_d = wfh_d if which == 0 else wfl_d
                    src = src_t[:].rearrange("p (c t) -> p c t", c=4)
                    dst = dst_d[r0:r0 + 512, mm * 128:(mm + 1) * 128].rearrange("(c p) t -> p c t", p=128)
                    return lambda raw: raw.dma_start(out=dst, in_=src)
                t_o1 = em("gp", mfo(m5, r5, 0), waits=[t_fh], inc="wfo", amt=16)
                t_o2 = em("gp", mfo(m5, r5, 1), waits=[t_fl], inc="wfo", amt=16)
                fs_free = t_o2
        wf_split_done = ("wfo", P.cnt["wfo"])

        ctxA.close()
        ctxB = ExitStack()
        EB = ctxB.enter_context
        wstg = EB(nc.sbuf_tensor("wstg", [128, 4 * 384], F32))
        wquph = EB(nc.sbuf_tensor("wquph", [128, 12 * 384], BF16))
        wqupl = EB(nc.sbuf_tensor("wqupl", [128, 12 * 384], BF16))
        wkvuph = EB(nc.sbuf_tensor("wkvuph", [128, 4 * 512], BF16))
        wkvupl = EB(nc.sbuf_tensor("wkvupl", [128, 4 * 512], BF16))
        qph = EB(nc.sbuf_tensor("qph", [128, 3 * OWN], BF16))
        qpl = EB(nc.sbuf_tensor("qpl", [128, 3 * OWN], BF16))
        knh = EB(nc.sbuf_tensor("knh", [128, S], BF16))
        knl = EB(nc.sbuf_tensor("knl", [128, S], BF16))
        vt = EB(nc.sbuf_tensor("vt", [128, 16 * VD], F32))
        Pex = EB(nc.sbuf_tensor("Pex", [128, S], F32))
        PT = EB(nc.sbuf_tensor("PT", [128, 16 * 256], F32))
        sums = EB(nc.sbuf_tensor("sums", [128, 8], F32))
        rrow = EB(nc.sbuf_tensor("rrow", [1, OWN], F32))
        rbs = EB(nc.sbuf_tensor("rbs", [128, 256], F32))
        negm = EB(nc.sbuf_tensor("negm", [128, 1], F32))
        rmax = EB(nc.sbuf_tensor("rmax", [128, 1], F32))
        wqup_free = [qc_ready, qc_ready]
        wkvup_free = [None, None]
        tok_wqup = [None, None]
        tok_wkvup = [None, None]

        wstg_free = [None]

        def phaseB_loads(hp):
            bi = 0
            c0 = hp * 2 * QH
            t_l1 = wstg_free[0]
            for hf in range(3):
                def msw(h6=hf, cc=c0):
                    src = d["wqupT"][512 * h6:512 * (h6 + 1), cc:cc + 384]
                    return dma3(wstg, src, 4, 384)
                t_s1 = em("sy", msw(), waits=[t_l1, wqup_free[bi]], inc="lqup", amt=16)
                def mswh(h6=hf):
                    def f(raw):
                        return raw.tensor_copy(wquph[:, 1536 * h6:1536 * (h6 + 1)], wstg[:])
                    return f
                t_h1 = em("ve", mswh(), waits=[t_s1, wqup_free[bi]], inc="ve")
                def mswl(h6=hf):
                    def f(raw):
                        return raw.tensor_tensor(wqupl[:, 1536 * h6:1536 * (h6 + 1)], wstg[:],
                                                 wquph[:, 1536 * h6:1536 * (h6 + 1)], AluOpType.subtract)
                    return f
                t_l1 = em("ve", mswl(), waits=[t_h1], inc="ve")
            tok_wqup[bi] = t_l1
            c1 = hp * 2 * (DN + VD)
            for hf in range(2):
                def msw2(h6=hf, cc=c1):
                    src = d["wkvupT"][256 * h6:256 * (h6 + 1), cc:cc + 512]
                    return dma3_into(wstg, 0, src, 2, 512)
                t_s2 = em("sy", msw2(), waits=[t_l1, wkvup_free[bi]], inc="lkvup", amt=16)
                def mswh2(h6=hf):
                    def f(raw):
                        return raw.tensor_copy(wkvuph[:, 1024 * h6:1024 * (h6 + 1)], wstg[:, 0:1024])
                    return f
                t_h2 = em("ve", mswh2(), waits=[t_s2, wkvup_free[bi]], inc="ve")
                def mswl2(h6=hf):
                    def f(raw):
                        return raw.tensor_tensor(wkvupl[:, 1024 * h6:1024 * (h6 + 1)], wstg[:, 0:1024],
                                                 wkvuph[:, 1024 * h6:1024 * (h6 + 1)], AluOpType.subtract)
                    return f
                t_l1 = em("ve", mswl2(), waits=[t_h2], inc="ve")
            tok_wkvup[bi] = t_l1
            wstg_free[0] = t_l1

        oacc_tok = em("ve", lambda raw: raw.memset(oacc[:], 0.0), inc="ve")

        qcb = qc.ap().bitcast(BF16)
        kvb = kvc.ap().bitcast(BF16)
        split_tok = qc_ready
        for src_t, bview, nsub in ((qc, qcb, 24), (kvc, kvb, 20)):
            lo_toks = [None] * nsub
            for c2 in range(nsub):
                prev = lo_toks[c2 - 4] if c2 >= 4 else None
                def mhi(t=src_t, bv=bview, c3=c2):
                    def f(raw):
                        return raw.tensor_copy(bv[:, c3 * 1024:c3 * 1024 + 512],
                                               t[:, 2048 + c3 * 512:2048 + (c3 + 1) * 512])
                    return f
                t_hi = em("ve", mhi(), waits=[qc_ready, prev], inc="ve")
                def mlo(t=src_t, bv=bview, c3=c2):
                    def f(raw):
                        return raw.tensor_tensor(bv[:, c3 * 1024 + 512:(c3 + 1) * 1024],
                                                 t[:, 2048 + c3 * 512:2048 + (c3 + 1) * 512],
                                                 bv[:, c3 * 1024:c3 * 1024 + 512],
                                                 AluOpType.subtract)
                    return f
                lo_toks[c2] = em("ve", mlo(), waits=[t_hi], inc="ve")
                split_tok = lo_toks[c2]
        qcs_ready = split_tok
        kvcs_ready = split_tok
        def qc_h(k, n):
            c3 = 2 * k + n
            return qcb[:, c3 * 1024:c3 * 1024 + 512]
        def qc_l(k, n):
            c3 = 2 * k + n
            return qcb[:, c3 * 1024 + 512:(c3 + 1) * 1024]
        def kv_h(m, n, p0=0, p1=128, w=512):
            c3 = 4 * m + n
            return kvb[p0:p1, c3 * 1024:c3 * 1024 + w]
        def kv_l(m, n, p0=0, p1=128, w=512):
            c3 = 4 * m + n
            return kvb[p0:p1, c3 * 1024 + 512:c3 * 1024 + 512 + w]

        phaseB_loads(0)
        qp_free = None
        kn_free = None
        vt_free = None
        pA_free = None
        pB_free = None
        pC_free = None
        pD_free = None
        pbig_free = qc_ready
        rbs_free = None
        rrow_free = None
        t_exp = None

        for hp in range(16):
            bi = 0
            qp_toks = []
            for m in range(3):
                for n in range(2):
                    bk = next_bank()
                    ws = [tok_wqup[bi], qcs_ready, bank_free[bk], qp_free]
                    tk = None
                    nmm = 0
                    for k in range(12):
                        for wv, xv in ((wquph, qc_h(k, n)), (wquph, qc_l(k, n)), (wqupl, qc_h(k, n))):
                            def mk(mm=m, kk=k, b=bk, w=wv, x=xv, first=(nmm == 0), last=(nmm == 35)):
                                def f(raw):
                                    return raw.matmul(pbanks[b][:],
                                                      w[:, kk * 384 + mm * 128: kk * 384 + (mm + 1) * 128],
                                                      x, start=first, stop=last)
                                return f
                            tk = em("te", mk(), waits=ws if nmm == 0 else (),
                                    inc="te" if nmm == 35 else None)
                            nmm += 1
                    dh = qph[:, m * OWN + n * 512: m * OWN + (n + 1) * 512]
                    dl = qpl[:, m * OWN + n * 512: m * OWN + (n + 1) * 512]
                    t_eh = em("ve", (lambda ds, b: lambda raw: raw.tensor_copy(ds, pbanks[b][:]))(dh, bk),
                              waits=[tk], inc="ve")
                    bank_free[bk] = em("ve",
                                       (lambda ds, b, hh3=dh: lambda raw: raw.tensor_tensor(ds, pbanks[b][:], hh3, AluOpType.subtract))(dl, bk),
                                       waits=[t_eh], inc="ve")
                    qp_toks.append(bank_free[bk])
            qp_ready = qp_toks[-1]
            qp_free = None

            for hh in range(2):
                kn_toks = []
                for n in range(4):
                    bk = next_bank()
                    ws = [tok_wkvup[bi], kvcs_ready, bank_free[bk], kn_free]
                    tk = None
                    nmm = 0
                    for k in range(4):
                        for wv, xv in ((wkvuph, kv_h(k, n)), (wkvuph, kv_l(k, n)), (wkvupl, kv_h(k, n))):
                            def mk(kk=k, b=bk, hh2=hh, w=wv, x=xv, first=(nmm == 0), last=(nmm == 11)):
                                def f(raw):
                                    return raw.matmul(pbanks[b][:],
                                                      w[:, kk * 512 + hh2 * 256: kk * 512 + hh2 * 256 + 128],
                                                      x, start=first, stop=last)
                                return f
                            tk = em("te", mk(), waits=ws if nmm == 0 else (),
                                    inc="te" if nmm == 11 else None)
                            nmm += 1
                    dh = knh[:, n * 512:(n + 1) * 512]
                    dl = knl[:, n * 512:(n + 1) * 512]
                    t_eh = em("ve", (lambda ds, b: lambda raw: raw.tensor_copy(ds, pbanks[b][:]))(dh, bk),
                              waits=[tk], inc="ve")
                    bank_free[bk] = em("ve",
                                       (lambda ds, b, hh3=dh: lambda raw: raw.tensor_tensor(ds, pbanks[b][:], hh3, AluOpType.subtract))(dl, bk),
                                       waits=[t_eh], inc="ve")
                    kn_toks.append(bank_free[bk])
                kn_ready = kn_toks[-1]
                kn_free = None

                vt_toks = []
                for tt in range(16):
                    bk = next_bank()
                    ws = [tok_wkvup[bi], kvcs_ready, bank_free[bk], vt_free]
                    tk = None
                    nmm = 0
                    for k in range(4):
                        kvh_s = kv_h(k, tt // 4, 0, 128, 512)
                        kvl_s = kv_l(k, tt // 4, 0, 128, 512)
                        off = (tt % 4) * 128
                        for lv, wv in ((kvh_s, wkvuph), (kvl_s, wkvuph), (kvh_s, wkvupl)):
                            def mk(kk=k, b=bk, hh2=hh, lh=lv, w=wv, o=off, first=(nmm == 0), last=(nmm == 11)):
                                def f(raw):
                                    return raw.matmul(pbanks[b][:, 0:128],
                                                      lh[:, o:o + 128],
                                                      w[:, kk * 512 + hh2 * 256 + 128: kk * 512 + hh2 * 256 + 256],
                                                      start=first, stop=last)
                                return f
                            tk = em("te", mk(), waits=ws if nmm == 0 else (),
                                    inc="te" if nmm == 11 else None)
                            nmm += 1
                    dstc = vt[:, tt * VD:(tt + 1) * VD]
                    bank_free[bk] = em("ve",
                                       (lambda ds, b: lambda raw: raw.tensor_copy(ds, pbanks[b][:, 0:128]))(dstc, bk),
                                       waits=[tk], inc="ve")
                    vt_toks.append(bank_free[bk])
                vt_ready = vt_toks[-1]
                vt_free = None

                def q_nope_hl(qt, hh2=hh):
                    a, b2 = hh2 * OWN + qt * 128, hh2 * OWN + (qt + 1) * 128
                    return qph[:, a:b2], qpl[:, a:b2]

                def q_rope_hl(qt, hh2=hh):
                    a, b2 = 2 * OWN + qt * 128, 2 * OWN + (qt + 1) * 128
                    if hh2 == 0:
                        return qph[0:64, a:b2], qpl[0:64, a:b2]
                    return qph[64:128, a:b2], qpl[64:128, a:b2]

                rope_lo = (hh == 0)

                for qg in range(4):
                    for qi in range(2):
                        qt = qg * 2 + qi
                        qah, qal = q_nope_hl(qt)
                        qbh, qbl = q_rope_hl(qt)
                        p0, p1 = (0, 64) if rope_lo else (64, 128)
                        sc_tok = None
                        ws0 = [qp_ready, kn_ready, vt_ready, pbig_free]
                        for kt in range(4):
                            ops = [(qah, knh[:, kt * 512:(kt + 1) * 512]),
                                   (qal, knh[:, kt * 512:(kt + 1) * 512]),
                                   (qah, knl[:, kt * 512:(kt + 1) * 512]),
                                   (qbh, kv_h(4, kt, p0, p1)),
                                   (qbl, kv_h(4, kt, p0, p1)),
                                   (qbh, kv_l(4, kt, p0, p1))]
                            for i2, (lh, rh) in enumerate(ops):
                                def mk1(k2=kt, a=lh, b2=rh, first=(i2 == 0), last=(i2 == 5)):
                                    def f(raw):
                                        return raw.matmul(pbig[:, k2 * 512:(k2 + 1) * 512],
                                                          a, b2, start=first, stop=last)
                                    return f
                                sc_tok = em("te", mk1(), waits=ws0 if (kt == 0 and i2 == 0) else (),
                                            inc="te" if i2 == 5 else None)
                        t_mx = em("ve", lambda raw: raw.reduce_max(rmax[:], pbig[:], axis=AX),
                                  waits=[sc_tok], inc="ve")
                        t_ng = em("ve", lambda raw: raw.tensor_scalar(negm[:], rmax[:], -INV_SQRT_QH, None, AluOpType.mult),
                                  waits=[t_mx], inc="ve")
                        def mkexp(q2=qt):
                            def f(raw):
                                return raw.activation(Pex[:], pbig[:], AF.Exp, bias=negm[:],
                                                      scale=INV_SQRT_QH, accum_out=sums[:, q2:q2 + 1])
                            return f
                        t_exp = em("se", mkexp(), waits=[t_ng], inc="se")
                        pbig_free = t_exp
                        for g4 in range(4):
                            even = (g4 % 2 == 0)
                            pbank = pA if even else pB
                            pfree = pA_free if even else pB_free
                            ttr = None
                            for j in range(4):
                                kt2 = g4 * 4 + j
                                def mtr(k3=kt2, jj=j, pb=pbank):
                                    def f(raw):
                                        return raw.matmul(pb[:, jj * 128:(jj + 1) * 128],
                                                          Pex[:, k3 * 128:(k3 + 1) * 128], idt[:],
                                                          start=(jj == 0), stop=(jj == 3),
                                                          is_transpose=True)
                                    return f
                                ttr = em("te", mtr(), waits=[t_exp, pfree, t_const] if j == 0 else (),
                                         inc="te" if j == 3 else None)
                            def mev(g5=g4, q2=qi, pb=pbank):
                                def f(raw):
                                    dst = PT.ap().rearrange("p (k q c) -> p k q c", q=2, c=128)[:, g5 * 4:(g5 + 1) * 4, q2, :]
                                    return raw.copy(dst, pb[:])
                                return f
                            tev = em("se", mev(), waits=[ttr], inc="se")
                            if even:
                                pA_free = tev
                            else:
                                pB_free = tev
                    pv_ws = [pA_free, pB_free, vt_ready, pC_free]
                    t_pv = None
                    for kt2 in range(16):
                        def mpv(k3=kt2):
                            def f(raw):
                                return raw.matmul(pC[:, 0:256],
                                                  vt[:, k3 * VD:(k3 + 1) * VD],
                                                  PT[:, k3 * 256:(k3 + 1) * 256],
                                                  start=(k3 == 0), stop=(k3 == 15))
                            return f
                        t_pv = em("te", mpv(), waits=pv_ws if kt2 == 0 else (),
                                  inc="te" if kt2 == 15 else None)
                    t_tr2 = None
                    for qi2 in range(2):
                        def mtr2(q3=qg * 2 + qi2, j2=qi2):
                            def f(raw):
                                return raw.matmul(pD[0:1, j2 * 128:(j2 + 1) * 128],
                                                  sums[:, q3:q3 + 1], idt[:],
                                                  start=(j2 == 0), stop=(j2 == 1),
                                                  is_transpose=True)
                            return f
                        t_tr2 = em("te", mtr2(), waits=[t_exp, t_const, (pD_free if qi2 == 0 else None)],
                                   inc="te" if qi2 == 1 else None)
                    def mrc(q2=qg):
                        def f(raw):
                            return raw.reciprocal(rrow[0:1, q2 * 256:(q2 + 1) * 256], pD[0:1, 0:256])
                        return f
                    t_rc = em("ve", mrc(), waits=[t_tr2, rrow_free], inc="ve")
                    def mbc(q2=qg):
                        def f(raw):
                            return raw.matmul(pD[:, 256:512], on1[:], rrow[0:1, q2 * 256:(q2 + 1) * 256],
                                              start=True, stop=True)
                        return f
                    t_bc = em("te", mbc(), waits=[t_rc, t_const], inc="te")
                    t_rb = em("se", lambda raw: raw.copy(rbs[:], pD[:, 256:512]),
                              waits=[t_bc, rbs_free], inc="se")
                    pD_free = t_rb
                    t_sc2 = em("ve", lambda raw: raw.tensor_tensor(rbs[:], pC[:, 0:256], rbs[:], AluOpType.mult),
                               waits=[t_pv, t_rb, oacc_tok], inc="ve")
                    pC_free = t_sc2
                    rbs_free = t_sc2
                    def macc(q2=qg):
                        def f(raw):
                            ds = oacc[:, q2 * 256:(q2 + 1) * 256]
                            return raw.tensor_tensor(ds, rbs[:], ds, AluOpType.add)
                        return f
                    oacc_tok = em("ve", macc(), waits=[t_sc2], inc="ve")
                    rrow_free = t_bc
                kn_free = sc_tok
                vt_free = t_pv
            qp_free = sc_tok
            wqup_free[bi] = qp_ready
            wkvup_free[bi] = vt_ready
            if hp + 1 < 16:
                phaseB_loads(hp + 1)

        ctxB.close()
        ctxAB.close()
        ctxC = ctx.enter_context(ExitStack())
        EC = ctxC.enter_context
        x2 = EC(nc.sbuf_tensor("x2", [128, 32 * CT], F32))
        yt_ = EC(nc.sbuf_tensor("yt_", [128, 32 * CT], F32))
        wos = EC(nc.sbuf_tensor("wos", [128, H], F32))
        wbs = EC(nc.sbuf_tensor("wbs", [64, H], F32))
        wstgC = EC(nc.sbuf_tensor("wstgC", [128, 32 * 128], F32))
        wash = EC(nc.sbuf_tensor("wash", [128, 32 * 64], BF16))
        wasl = EC(nc.sbuf_tensor("wasl", [128, 32 * 64], BF16))
        wgsh = EC(nc.sbuf_tensor("wgsh", [128, 32 * 128], BF16))
        wgsl = EC(nc.sbuf_tensor("wgsl", [128, 32 * 128], BF16))
        wfh = [EC(nc.sbuf_tensor(f"wfh{i}", [128, 32 * 128], BF16)) for i in range(2)]
        wfl = [EC(nc.sbuf_tensor(f"wfl{i}", [128, 32 * 128], BF16)) for i in range(2)]
        scrC = EC(nc.sbuf_tensor("scrC", [128, CT], F32))
        xoc = [EC(nc.sbuf_tensor(f"xoc{i}", [128, CT], F32)) for i in range(2)]
        sqs = EC(nc.sbuf_tensor("sqs", [128, CT], F32))
        us = EC(nc.sbuf_tensor("us", [64, CT], F32))
        ums = EC(nc.sbuf_tensor("ums", [64, CT], F32))
        srow = EC(nc.sbuf_tensor("srow", [1, 4 * CT], F32))
        arow = EC(nc.sbuf_tensor("arow", [1, 4 * CT], F32))
        lrow = EC(nc.sbuf_tensor("lrow", [1, 4 * CT], F32))
        mub = EC(nc.sbuf_tensor("mub", [128, CT], F32))
        isb = EC(nc.sbuf_tensor("isb", [128, CT], F32))
        em("sy", dma(wos[:], d["woT"]), waits=[oacc_tok], inc="ldc", amt=16)
        em("sy", dma(wbs[:], d["wbT"]), waits=[oacc_tok], inc="ldc", amt=16)
        t_sa = em("sy", dma3_into(wstgC, 0, d["waT"], 32, 64), waits=[oacc_tok], inc="ldc", amt=16)
        t_ah = em("ve", lambda raw: raw.tensor_copy(wash[:], wstgC[:, 0:2048]), waits=[t_sa], inc="ve")
        t_al = em("ve", lambda raw: raw.tensor_tensor(wasl[:], wstgC[:, 0:2048], wash[:], AluOpType.subtract),
                  waits=[t_ah], inc="ve")
        t_sg = em("sy", dma3(wstgC, d["wgP"], 32, 128), waits=[t_al], inc="ldc", amt=16)
        t_gh = em("ve", lambda raw: raw.tensor_copy(wgsh[:], wstgC[:]), waits=[t_sg], inc="ve")
        t_gl = em("ve", lambda raw: raw.tensor_tensor(wgsl[:], wstgC[:], wgsh[:], AluOpType.subtract),
                  waits=[t_gh], inc="ve")
        t_const = ("ldc", 64)
        t_wsplit = t_gl
        wfs_free = [oacc_tok, oacc_tok]
        tok_wfs = [None, None]
        xoc_free = [oacc_tok, oacc_tok]
        tok_xoc = [None, None]
        x2_free = None
        yt_free = None
        pr_free = {"pA": pA_free, "pB": pB_free, "pC": pC_free, "pD": pD_free}
        pbig_free_l = [oacc_tok] * 4
        mub_free = None
        isb_free = None
        lrow_free = None
        arow_free = None
        srow_free = None
        ums_free = None
        us_free = None
        sqs_free = None

        L = lambda i: lrow[0:1, i * CT:(i + 1) * CT]
        SR = lambda g: srow[0:1, g * CT:(g + 1) * CT]
        AR = lambda i: arow[0:1, i * CT:(i + 1) * CT]

        def ln_rows(s1_tok, s2_tok):
            t_mu = em("ve", lambda raw: raw.tensor_scalar(L(0), pC[0:1, 0:CT], 1.0 / H, None, AluOpType.mult),
                      waits=[s1_tok, lrow_free], inc="ve")
            t_ms = em("ve", lambda raw: raw.tensor_scalar(L(1), pD[0:1, 0:CT], 1.0 / H, None, AluOpType.mult),
                      waits=[s2_tok], inc="ve")
            t_m2 = em("ve", lambda raw: raw.tensor_tensor(L(2), L(0), L(0), AluOpType.mult),
                      waits=[t_mu], inc="ve")
            t_va = em("ve", lambda raw: raw.tensor_tensor(L(2), L(1), L(2), AluOpType.subtract),
                      waits=[t_ms, t_m2], inc="ve")
            t_ve2 = em("ve", lambda raw: raw.tensor_scalar(L(2), L(2), 1e-5, None, AluOpType.add),
                       waits=[t_va], inc="ve")
            t_ln = em("se", lambda raw: raw.activation(L(3), L(2), AF.Ln), waits=[t_ve2], inc="se")
            t_is = em("se", lambda raw: raw.activation(L(3), L(3), AF.Exp, scale=-0.5), waits=[t_ln], inc="se")
            pr_free["pC"] = t_mu
            pr_free["pD"] = t_ms
            t_b1 = em("te", lambda raw: raw.matmul(pA[:, 0:CT], on1[:], L(0), start=True, stop=True),
                      waits=[t_mu, t_const, pr_free["pA"]], inc="te")
            t_c1 = em("se", lambda raw: raw.copy(mub[:], pA[:, 0:CT]), waits=[t_b1, mub_free], inc="se")
            pr_free["pA"] = t_c1
            t_b2 = em("te", lambda raw: raw.matmul(pB[:, 0:CT], on1[:], L(3), start=True, stop=True),
                      waits=[t_is, t_const, pr_free["pB"]], inc="te")
            t_c2 = em("se", lambda raw: raw.copy(isb[:], pB[:, 0:CT]), waits=[t_b2, isb_free], inc="se")
            pr_free["pB"] = t_c2
            return t_c1, t_c2, t_b2

        for ct in range(NCT):
            tcol = ct * CT
            s1_tok = None
            s2_tok = None
            x2_toks = []
            for m in range(32):
                bk = m % 2
                def mwo(mm=m, b=bk, c0=tcol):
                    def f(raw):
                        return raw.matmul(pbanks[b][:, 0:CT], wos[:, mm * 128:(mm + 1) * 128],
                                          oacc[:, c0:c0 + CT], start=True, stop=True)
                    return f
                t_wo = em("te", mwo(), waits=[oacc_tok, t_const, pbig_free_l[bk]], inc="te")
                xb = m % 2
                def mxo(mm=m, b2=xb, c0=tcol):
                    def f(raw):
                        return raw.dma_start(out=xoc[b2][:], in_=d["xTown"][mm * 128:(mm + 1) * 128, c0:c0 + CT])
                    return f
                tok_xoc[xb] = em("sy", mxo(), waits=[xoc_free[xb]], inc=("lxo1" if xb else "lxo0"), amt=16)
                def mev2(mm=m, b=bk, b2=xb):
                    def f(raw):
                        return raw.tensor_tensor(x2[:, mm * CT:(mm + 1) * CT], pbanks[b][:, 0:CT],
                                                 xoc[b2][:], AluOpType.add)
                    return f
                t_x2 = em("ve", mev2(), waits=[t_wo, tok_xoc[xb], (x2_free if m == 0 else None)], inc="ve")
                x2_toks.append(t_x2)
                xoc_free[xb] = t_x2
                pbig_free_l[bk] = t_x2
                def ms1(mm=m):
                    def f(raw):
                        return raw.matmul(pC[0:1, 0:CT], onc[:], x2[:, mm * CT:(mm + 1) * CT],
                                          start=(mm == 0), stop=(mm == 31))
                    return f
                s1_tok = em("te", ms1(), waits=[t_x2, t_const, (pr_free["pC"] if m == 0 else None)],
                            inc="te" if m == 31 else None)
                def msq(mm=m):
                    def f(raw):
                        return raw.square(sqs[:], x2[:, mm * CT:(mm + 1) * CT])
                    return f
                t_sq = em("se", msq(), waits=[t_x2, sqs_free], inc="se")
                def ms2(mm=m):
                    def f(raw):
                        return raw.matmul(pD[0:1, 0:CT], onc[:], sqs[:],
                                          start=(mm == 0), stop=(mm == 31))
                    return f
                s2_tok = em("te", ms2(), waits=[t_sq, (pr_free["pD"] if m == 0 else None)],
                            inc="te" if m == 31 else None)
                sqs_free = s2_tok
            t_c1, t_c2, _ = ln_rows(s1_tok, s2_tok)
            lrow_free = pr_free["pB"]
            x2n_toks = []
            for m in range(32):
                def mn1(mm=m):
                    def f(raw):
                        ds = x2[:, mm * CT:(mm + 1) * CT]
                        return raw.tensor_tensor(ds, ds, mub[:], AluOpType.subtract)
                    return f
                t_n1 = em("ve", mn1(), waits=[t_c1, x2_toks[m]], inc="ve")
                def mn2(mm=m):
                    def f(raw):
                        ds = x2[:, mm * CT:(mm + 1) * CT]
                        return raw.tensor_tensor(ds, ds, isb[:], AluOpType.mult)
                    return f
                t_n2 = em("ve", mn2(), waits=[t_c2, t_n1], inc="ve")
                x2n_toks.append(t_n2)
            mub_free = x2n_toks[-1]
            isb_free = x2n_toks[-1]
            x2b = x2.ap().bitcast(BF16)
            scrC_free = None
            sp_tok = None
            for m in range(32):
                def mcp2(mm=m):
                    def f(raw):
                        return raw.tensor_copy(scrC[:], x2[:, mm * CT:(mm + 1) * CT])
                    return f
                t_cp2 = em("ve", mcp2(), waits=[x2n_toks[m], scrC_free], inc="ve")
                def mhi2(mm=m):
                    def f(raw):
                        return raw.tensor_copy(x2b[:, mm * 2 * CT:mm * 2 * CT + CT], scrC[:])
                    return f
                t_hi2 = em("ve", mhi2(), waits=[t_cp2], inc="ve")
                def mlo2(mm=m):
                    def f(raw):
                        return raw.tensor_tensor(x2b[:, mm * 2 * CT + CT:(mm + 1) * 2 * CT],
                                                 scrC[:], x2b[:, mm * 2 * CT:mm * 2 * CT + CT],
                                                 AluOpType.subtract)
                    return f
                sp_tok = em("ve", mlo2(), waits=[t_hi2], inc="ve")
                scrC_free = sp_tok
            x2_ready = sp_tok
            def x2_h(k):
                return x2b[:, k * 2 * CT:k * 2 * CT + CT]
            def x2_l(k):
                return x2b[:, k * 2 * CT + CT:(k + 1) * 2 * CT]

            g_tok = None
            nmm = 0
            for k in range(32):
                for wv, xv in ((wgsh, x2_h(k)), (wgsh, x2_l(k)), (wgsl, x2_h(k))):
                    def mg(kk=k, w=wv, x=xv, first=(nmm == 0), last=(nmm == 95)):
                        def f(raw):
                            return raw.matmul(pC[:, 0:CT], w[:, kk * 128:(kk + 1) * 128],
                                              x, start=first, stop=last)
                        return f
                    g_tok = em("te", mg(), waits=([x2_ready, t_const, t_wsplit, pr_free["pC"]] if nmm == 0 else ()),
                               inc="te" if nmm == 95 else None)
                    nmm += 1
            s_cp = []
            for g in range(4):
                def msg(gg=g):
                    def f(raw):
                        return raw.copy(srow[0:1, gg * CT:(gg + 1) * CT], pC[32 * gg:32 * gg + 1, 0:CT])
                    return f
                s_cp.append(em("se", msg(), waits=[g_tok, srow_free] if g == 0 else [g_tok], inc="se"))
            s_ad = []
            for g in range(4):
                def msa(gg=g):
                    def f(raw):
                        return raw.tensor_scalar(SR(gg), SR(gg), gbs[0:1, gg:gg + 1], None, AluOpType.add)
                    return f
                s_ad.append(em("ve", msa(), waits=[s_cp[g]], inc="ve"))
            pr_free["pC"] = s_cp[-1]
            s_tok = s_ad[-1]
            t1 = em("ve", lambda raw: raw.tensor_tensor(AR(0), SR(2), SR(3), AluOpType.max),
                    waits=[s_tok, arow_free], inc="ve")
            t2 = em("ve", lambda raw: raw.tensor_tensor(AR(1), SR(1), AR(0), AluOpType.max),
                    waits=[t1], inc="ve")
            t3 = em("ve", lambda raw: raw.tensor_tensor(AR(1), AR(1), SR(0), AluOpType.is_gt),
                    waits=[t2], inc="ve")
            t4 = em("ve", lambda raw: raw.tensor_tensor(AR(2), SR(0), SR(1), AluOpType.max),
                    waits=[t3], inc="ve")
            t5 = em("ve", lambda raw: raw.tensor_tensor(AR(2), AR(0), AR(2), AluOpType.is_gt),
                    waits=[t4], inc="ve")
            t6 = em("ve", lambda raw: raw.tensor_tensor(AR(0), SR(0), SR(1), AluOpType.max),
                    waits=[t5], inc="ve")
            t7 = em("ve", lambda raw: raw.tensor_tensor(AR(0), AR(0), SR(2), AluOpType.max),
                    waits=[t6], inc="ve")
            t8 = em("ve", lambda raw: raw.tensor_tensor(AR(0), SR(3), AR(0), AluOpType.is_gt),
                    waits=[t7], inc="ve")
            srow_free = t8
            em("te", lambda raw: raw.matmul(pB[:, 0:CT], sgp[0:1, 0:128], onr[0:1, 0:CT], start=True, stop=False),
               waits=[t8, t_const, pr_free["pB"]])
            em("te", lambda raw: raw.matmul(pB[:, 0:CT], sgp[0:1, 128:256], AR(1), start=False, stop=False))
            em("te", lambda raw: raw.matmul(pB[:, 0:CT], sgp[0:1, 256:384], AR(2), start=False, stop=False))
            m_tok = em("te", lambda raw: raw.matmul(pB[:, 0:CT], sgp[0:1, 384:512], AR(0), start=False, stop=True),
                       inc="te")
            arow_free = m_tok
            u_tok = None
            nmm = 0
            for k in range(32):
                for wv, xv in ((wash, x2_h(k)), (wash, x2_l(k)), (wasl, x2_h(k))):
                    def mu(kk=k, w=wv, x=xv, first=(nmm == 0), last=(nmm == 95)):
                        def f(raw):
                            return raw.matmul(pA[0:64, 0:CT], w[:, kk * 64:(kk + 1) * 64],
                                              x, start=first, stop=last)
                        return f
                    u_tok = em("te", mu(), waits=([x2_ready, t_const, t_wsplit, pr_free["pA"]] if nmm == 0 else ()),
                               inc="te" if nmm == 95 else None)
                    nmm += 1
            t_ue = em("se", lambda raw: raw.copy(us[:], pA[0:64, 0:CT]), waits=[u_tok, us_free], inc="se")
            pr_free["pA"] = t_ue
            t_um = em("ve", lambda raw: raw.tensor_tensor(ums[:], us[:], pB[0:64, 0:CT], AluOpType.mult),
                      waits=[t_ue, m_tok, ums_free], inc="ve")
            us_free = t_um
            pr_free["pB"] = t_um

            s1_tok = None
            s2_tok = None
            y_toks = []
            t_d = None
            for m in range(32):
                wb = m % 2
                def mwfh(mm=m, b2=wb):
                    return dma3(wfh[b2], wfh_d[:, mm * 128:(mm + 1) * 128], 32, 128)
                def mwfl(mm=m, b2=wb):
                    return dma3(wfl[b2], wfl_d[:, mm * 128:(mm + 1) * 128], 32, 128)
                t_l1 = em("sy", mwfh(), waits=[wfs_free[wb], wf_split_done], inc=("lwf1" if wb else "lwf0"), amt=16)
                tok_wfs[wb] = em("sy", mwfl(), waits=[t_l1], inc=("lwf1" if wb else "lwf0"), amt=16)
                bk = 2 + (m % 2)
                nmm = 0
                for k in range(32):
                    for wv, xv in ((wfh[wb], x2_h(k)), (wfh[wb], x2_l(k)), (wfl[wb], x2_h(k))):
                        def md(kk=k, b=bk, w=wv, x=xv, first=(nmm == 0)):
                            def f(raw):
                                return raw.matmul(pbanks[b][:, 0:CT], w[:, kk * 128:(kk + 1) * 128],
                                                  x, start=first, stop=False)
                            return f
                        em("te", md(), waits=[t_l1, tok_wfs[wb], x2_ready, t_um, pbig_free_l[bk]] if nmm == 0 else ())
                        nmm += 1
                def mdb(mm=m, b=bk):
                    def f(raw):
                        return raw.matmul(pbanks[b][:, 0:CT], wbs[:, mm * 128:(mm + 1) * 128],
                                          ums[:], start=False, stop=True)
                    return f
                t_d = em("te", mdb(), inc="te")
                wfs_free[wb] = t_d
                def mye(mm=m, b=bk):
                    def f(raw):
                        return raw.scalar_tensor_tensor(yt_[:, mm * CT:(mm + 1) * CT],
                                                        pbanks[b][:, 0:CT], bfs[:, mm:mm + 1],
                                                        x2_h(mm), AluOpType.add, AluOpType.add)
                    return f
                t_y0 = em("ve", mye(), waits=[t_d, (yt_free if m == 0 else None)], inc="ve")
                def mye2(mm=m):
                    def f(raw):
                        ds = yt_[:, mm * CT:(mm + 1) * CT]
                        return raw.tensor_tensor(ds, ds, x2_l(mm), AluOpType.add)
                    return f
                t_y = em("ve", mye2(), waits=[t_y0], inc="ve")
                y_toks.append(t_y)
                pbig_free_l[bk] = t_y
                def ms1b(mm=m):
                    def f(raw):
                        return raw.matmul(pC[0:1, 0:CT], onc[:], yt_[:, mm * CT:(mm + 1) * CT],
                                          start=(mm == 0), stop=(mm == 31))
                    return f
                s1_tok = em("te", ms1b(), waits=[t_y, t_const, (pr_free["pC"] if m == 0 else None)],
                            inc="te" if m == 31 else None)
                def msqb(mm=m):
                    def f(raw):
                        return raw.square(sqs[:], yt_[:, mm * CT:(mm + 1) * CT])
                    return f
                t_sq = em("se", msqb(), waits=[t_y, sqs_free], inc="se")
                def ms2b(mm=m):
                    def f(raw):
                        return raw.matmul(pD[0:1, 0:CT], onc[:], sqs[:],
                                          start=(mm == 0), stop=(mm == 31))
                    return f
                s2_tok = em("te", ms2b(), waits=[t_sq, (pr_free["pD"] if m == 0 else None)],
                            inc="te" if m == 31 else None)
                sqs_free = s2_tok
            x2_free = y_toks[-1]
            ums_free = t_d
            t_c1, t_c2, _ = ln_rows(s1_tok, s2_tok)
            lrow_free = pr_free["pB"]
            out_toks = []
            for m in range(32):
                def mn1(mm=m):
                    def f(raw):
                        ds = yt_[:, mm * CT:(mm + 1) * CT]
                        return raw.tensor_tensor(ds, ds, mub[:], AluOpType.subtract)
                    return f
                t_n1 = em("ve", mn1(), waits=[t_c1, y_toks[m]], inc="ve")
                def mn2(mm=m):
                    def f(raw):
                        ds = yt_[:, mm * CT:(mm + 1) * CT]
                        return raw.tensor_tensor(ds, ds, isb[:], AluOpType.mult)
                    return f
                t_n2 = em("ve", mn2(), waits=[t_c2, t_n1], inc="ve")
                def mst(mm=m, c0=tcol):
                    def f(raw):
                        return raw.dma_start(out=yT[mm * 128:(mm + 1) * 128, c0:c0 + CT],
                                             in_=yt_[:, mm * CT:(mm + 1) * CT])
                    return f
                out_toks.append(em("gp", mst(), waits=[t_n2], inc="out", amt=16))
            yt_free = out_toks[-1]
            mub_free = out_toks[-1]
            isb_free = out_toks[-1]

        em("gp", lambda raw: raw.nop(), waits=[("out", P.cnt["out"])])

        @block.tensor
        def _(tensor):
            P.replay("te", tensor, sems)

        @block.vector
        def _(vector):
            P.replay("ve", vector, sems)

        @block.scalar
        def _(scalar):
            P.replay("se", scalar, sems)

        @block.sync
        def _(sync):
            P.replay("sy", sync, sems)

        @block.gpsimd
        def _(gpsimd):
            P.replay("gp", gpsimd, sems)

    return nc


def _prep_inputs(L0_attn_hidden_size, L0_attn_Wq_c, L0_attn_Wq_up, L0_attn_Wkv_c,
                 L0_attn_Wkv_up, L0_attn_Wo, L0_ffn_W, L0_ffn_b, L0_ffn_A, L0_ffn_B,
                 L0_ffn_gate_w, L0_ffn_gate_b):
    c = np.ascontiguousarray
    x = np.asarray(L0_attn_hidden_size, dtype=np.float32)
    wqcT = c(np.asarray(L0_attn_Wq_c, np.float32).T)
    wqup = np.asarray(L0_attn_Wq_up, np.float32)
    wq3 = wqup.reshape(NH, QH, QL)
    cols = []
    for hp in range(NH // 2):
        h0, h1 = 2 * hp, 2 * hp + 1
        cols.append(wq3[h0, :DN])
        cols.append(wq3[h1, :DN])
        cols.append(wq3[h0, DN:])
        cols.append(wq3[h1, DN:])
    wqupT = c(np.concatenate(cols, axis=0).T)
    wkvc = np.asarray(L0_attn_Wkv_c, np.float32)
    wkvc_p = np.zeros((640, H), np.float32)
    wkvc_p[:576] = wkvc
    wkvc_p[576:640] = wkvc[512:576]
    wkvcT = c(wkvc_p.T)
    wkvupT = c(np.asarray(L0_attn_Wkv_up, np.float32).T)
    woT = c(np.asarray(L0_attn_Wo, np.float32).T)
    wffnT = c(np.asarray(L0_ffn_W, np.float32).T)
    waT = c(np.asarray(L0_ffn_A, np.float32).T)
    wbT = c(np.asarray(L0_ffn_B, np.float32).T)
    wg = np.asarray(L0_ffn_gate_w, np.float32)
    wgP = np.zeros((H, 128), np.float32)
    for g in range(4):
        wgP[:, 32 * g] = wg[g]
    bffn = c(np.asarray(L0_ffn_b, np.float32).reshape(32, 128).T)
    gateb = c(np.asarray(L0_ffn_gate_b, np.float32).reshape(1, 4))
    ident = np.eye(128, dtype=np.float32)
    ones1 = np.ones((1, 128), np.float32)
    onesc = np.ones((128, 1), np.float32)
    onesr = np.ones((1, 512), np.float32)
    segp = np.zeros((1, 512), np.float32)
    segp[0, 0:8] = 1.0
    segp[0, 128 + 8:128 + 16] = 1.0
    segp[0, 256 + 16:256 + 32] = 1.0
    segp[0, 384 + 32:384 + 64] = 1.0
    shared = dict(wqcT=wqcT, wqupT=wqupT, wkvcT=wkvcT, wkvupT=wkvupT, woT=woT,
                  wffnT=wffnT, waT=waT, wbT=wbT, wgP=wgP, bffn=bffn, gateb=gateb,
                  ident=ident, ones1=ones1, onesc=onesc, onesr=onesr, segp=segp)
    in_maps = []
    for core in range(8):
        b = core // 2
        off = (core % 2) * OWN
        xT = c(x[b].T)
        m = dict(shared)
        m["xTfull"] = xT
        m["xTown"] = c(xT[:, off:off + OWN])
        in_maps.append(m)
    return in_maps


def kernel(L0_attn_hidden_size, L0_attn_Wq_c, L0_attn_bq_c, L0_attn_Wq_up, L0_attn_bq_up,
           L0_attn_Wkv_c, L0_attn_bkv_c, L0_attn_Wkv_up, L0_attn_bkv_up, L0_attn_Wo,
           L0_attn_bo, L0_ffn_W, L0_ffn_b, L0_ffn_A, L0_ffn_B, L0_ffn_gate_w, L0_ffn_gate_b):
    in_maps = _prep_inputs(L0_attn_hidden_size, L0_attn_Wq_c, L0_attn_Wq_up, L0_attn_Wkv_c,
                           L0_attn_Wkv_up, L0_attn_Wo, L0_ffn_W, L0_ffn_b, L0_ffn_A,
                           L0_ffn_B, L0_ffn_gate_w, L0_ffn_gate_b)
    nc = build_nc()
    res = run_bass_kernel_spmd(nc, in_maps, core_ids=list(range(8)))
    out = np.empty((B, S, H), dtype=np.float32)
    for core in range(8):
        b = core // 2
        off = (core % 2) * OWN
        out[b, off:off + OWN, :] = res.results[core]["yT"].T
    return out


# version 07 (final)
# speedup vs baseline: 1.2535x; 1.2535x over previous
import numpy as np
from contextlib import ExitStack
import concourse.bass as bass
import concourse.mybir as mybir
from concourse.bass_utils import run_bass_kernel_spmd
from concourse.alu_op_type import AluOpType

F32 = mybir.dt.float32
BF16 = mybir.dt.bfloat16
AF = mybir.ActivationFunctionType
AX = mybir.AxisListType.X

B, S, H = 4, 2048, 4096
NH = 32
QL, KVL = 1536, 512
DN, DR = 128, 64
QH = DN + DR
VD = 128
OWN = 1024
INV_SQRT_QH = float(1.0 / np.sqrt(QH))

A_PASSES = 8
A_CH = 4
CT = 256
NCT = OWN // CT


class Prog:

    def __init__(self):
        self.ops = []
        self.cnt = {}

    def emit(self, eng, fn, waits=(), inc=None, amt=1):
        ws = [w for w in waits if w is not None]
        incs = []
        if inc is not None:
            self.cnt[inc] = self.cnt.get(inc, 0) + amt
            incs = [(inc, amt)]
        self.ops.append((eng, fn, ws, incs))
        return (inc, self.cnt[inc]) if inc is not None else None

    def replay(self, eng_name, raw, sems):
        water = {}
        for eng, fn, waits, incs in self.ops:
            if eng != eng_name:
                continue
            for s, v in waits:
                if water.get(s, 0) < v:
                    raw.wait_ge(sems[s], v)
                    water[s] = v
            inst = fn(raw)
            for s, amt in incs:
                inst.then_inc(sems[s], amt)


def build_nc():
    nc = bass.Bass("TRN2", target_bir_lowering=False)
    d = {}
    d["xTfull"] = nc.dram_tensor("xTfull", [H, S], F32, kind="ExternalInput").ap()
    d["xTown"] = nc.dram_tensor("xTown", [H, OWN], F32, kind="ExternalInput").ap()
    d["wqcT"] = nc.dram_tensor("wqcT", [H, QL], F32, kind="ExternalInput").ap()
    d["wqupT"] = nc.dram_tensor("wqupT", [QL, NH * QH], F32, kind="ExternalInput").ap()
    d["wkvcT"] = nc.dram_tensor("wkvcT", [H, 640], F32, kind="ExternalInput").ap()
    d["wkvupT"] = nc.dram_tensor("wkvupT", [KVL, NH * (DN + VD)], F32, kind="ExternalInput").ap()
    d["woT"] = nc.dram_tensor("woT", [VD, H], F32, kind="ExternalInput").ap()
    d["wffnT"] = nc.dram_tensor("wffnT", [H, H], F32, kind="ExternalInput").ap()
    d["waT"] = nc.dram_tensor("waT", [H, 64], F32, kind="ExternalInput").ap()
    d["wbT"] = nc.dram_tensor("wbT", [64, H], F32, kind="ExternalInput").ap()
    d["wgP"] = nc.dram_tensor("wgP", [H, 128], F32, kind="ExternalInput").ap()
    d["bffn"] = nc.dram_tensor("bffn", [128, 32], F32, kind="ExternalInput").ap()
    d["gateb"] = nc.dram_tensor("gateb", [1, 4], F32, kind="ExternalInput").ap()
    d["ident"] = nc.dram_tensor("ident", [128, 128], F32, kind="ExternalInput").ap()
    d["ones1"] = nc.dram_tensor("ones1", [1, 128], F32, kind="ExternalInput").ap()
    d["onesc"] = nc.dram_tensor("onesc", [128, 1], F32, kind="ExternalInput").ap()
    d["onesr"] = nc.dram_tensor("onesr", [1, 512], F32, kind="ExternalInput").ap()
    d["segp"] = nc.dram_tensor("segp", [1, 512], F32, kind="ExternalInput").ap()
    yT = nc.dram_tensor("yT", [H, OWN], F32, kind="ExternalOutput").ap()

    P = Prog()

    with ExitStack() as ctx:
        E = ctx.enter_context
        oacc = E(nc.sbuf_tensor("oacc", [128, OWN], F32))
        bfs = E(nc.sbuf_tensor("bfs", [128, 32], F32))
        gbs = E(nc.sbuf_tensor("gbs", [1, 4], F32))
        idt = E(nc.sbuf_tensor("idt", [128, 128], F32))
        on1 = E(nc.sbuf_tensor("on1", [1, 128], F32))
        onc = E(nc.sbuf_tensor("onc", [128, 1], F32))
        onr = E(nc.sbuf_tensor("onr", [1, 512], F32))
        sgp = E(nc.sbuf_tensor("sgp", [1, 512], F32))
        pbig = E(nc.psum_tensor("pbig", [128, 2048], F32))
        pA = E(nc.psum_tensor("pA", [128, 512], F32))
        pB = E(nc.psum_tensor("pB", [128, 512], F32))
        pC = E(nc.psum_tensor("pC", [128, 512], F32))
        pD = E(nc.psum_tensor("pD", [128, 512], F32))
        sems = {s: E(nc.semaphore(s)) for s in ("ld0", "ldc", "lxsl", "lxso", "lwqc", "lwkvc", "lqup", "lkvup", "lxo0", "lxo1", "lwf0", "lwf1", "wfo", "wfi", "out", "te", "ve", "se")}
        block = E(nc.Block())
        ctxAB = ctx.enter_context(ExitStack())
        EAB = ctxAB.enter_context
        qc = EAB(nc.sbuf_tensor("qc", [128, 2048 + 12 * OWN], F32))
        kvc = EAB(nc.sbuf_tensor("kvc", [128, 2048 + 5 * S], F32))
        ctxA = ExitStack()
        EA = ctxA.enter_context
        xsl = [EA(nc.sbuf_tensor("xsl0", [128, A_CH * S], F32))]
        xso = [EA(nc.sbuf_tensor("xso0", [128, A_CH * OWN], F32))]
        wqcs = [EA(nc.sbuf_tensor("wqcs0", [128, A_CH * QL], F32))]
        wkvcs = [EA(nc.sbuf_tensor("wkvcs0", [128, A_CH * 640], F32))]

        em = P.emit

        def dma(dst, src):
            return lambda raw: raw.dma_start(out=dst, in_=src)

        def dma3(dst_t, src_ap, nch, w):
            dst = dst_t[:].rearrange("p (c t) -> p c t", c=nch)
            src = src_ap.rearrange("(c p) t -> p c t", p=128)
            return lambda raw: raw.dma_start(out=dst, in_=src)

        def dma3_into(dst_t, col0, src_ap, nch, w):
            dst = dst_t[:, col0:col0 + nch * w].rearrange("p (c t) -> p c t", c=nch)
            src = src_ap.rearrange("(c p) t -> p c t", p=128)
            return lambda raw: raw.dma_start(out=dst, in_=src)

        t_const = None
        for dst, src in ((idt[:], d["ident"]), (on1[:], d["ones1"]), (onc[:], d["onesc"]),
                         (onr[:], d["onesr"]), (sgp[:], d["segp"]), (bfs[:], d["bffn"]),
                         (gbs[:], d["gateb"])):
            t_const = em("sy", dma(dst, src), inc="ld0", amt=16)

        pbanks = [pbig[:, i * 512:(i + 1) * 512] for i in range(4)]
        bank_free = [None, None, None, None]
        bank_i = [0]

        def next_bank():
            i = bank_i[0] % 4
            bank_i[0] += 1
            return i

        xsl_free = [None, None]
        xso_free = [None, None]
        wqc_free = [None, None]
        wkvc_free = [None, None]
        tok_xsl = [None, None]
        tok_xso = [None, None]
        tok_wqc = [None, None]
        tok_wkvc = [None, None]

        def phaseA_loads(p):
            bi = 0
            r0 = p * A_CH * 128
            tok_xsl[bi] = em("sy", dma3(xsl[bi], d["xTfull"][r0:r0 + A_CH * 128, :], A_CH, S),
                             waits=[xsl_free[bi]], inc="lxsl", amt=16)
            tok_xso[bi] = em("sy", dma3(xso[bi], d["xTown"][r0:r0 + A_CH * 128, :], A_CH, OWN),
                             waits=[xso_free[bi]], inc="lxso", amt=16)
            tok_wqc[bi] = em("sy", dma3(wqcs[bi], d["wqcT"][r0:r0 + A_CH * 128, :], A_CH, QL),
                             waits=[wqc_free[bi]], inc="lwqc", amt=16)
            tok_wkvc[bi] = em("sy", dma3(wkvcs[bi], d["wkvcT"][r0:r0 + A_CH * 128, :], A_CH, 640),
                              waits=[wkvc_free[bi]], inc="lwkvc", amt=16)

        phaseA_loads(0)
        for p in range(A_PASSES):
            bi = 0
            for m in range(12):
                for n in range(2):
                    bk = next_bank()
                    ws = [tok_wqc[bi], tok_xso[bi], bank_free[bk]]
                    tk = None
                    for k in range(A_CH):
                        def mk(mm=m, nn=n, kk=k, b=bk, bb=bi):
                            def f(raw):
                                return raw.matmul(pbanks[b][:],
                                                  wqcs[bb][:, kk * QL + mm * 128: kk * QL + (mm + 1) * 128],
                                                  xso[bb][:, kk * OWN + nn * 512: kk * OWN + (nn + 1) * 512],
                                                  start=(kk == 0), stop=(kk == A_CH - 1))
                            return f
                        tk = em("te", mk(), waits=ws if k == 0 else (),
                                inc="te" if k == A_CH - 1 else None)
                    dstc = qc[:, 2048 + m * OWN + n * 512: 2048 + m * OWN + (n + 1) * 512]
                    if p == 0:
                        fn = (lambda ds, b: lambda raw: raw.tensor_copy(ds, pbanks[b][:]))(dstc, bk)
                    else:
                        fn = (lambda ds, b: lambda raw: raw.tensor_tensor(ds, pbanks[b][:], ds, AluOpType.add))(dstc, bk)
                    bank_free[bk] = em("ve", fn, waits=[tk], inc="ve")
            for m in range(5):
                mw = 128
                for n in range(4):
                    bk = next_bank()
                    ws = [tok_wkvc[bi], tok_xsl[bi], bank_free[bk]]
                    tk = None
                    for k in range(A_CH):
                        def mk(mm=m, nn=n, kk=k, b=bk, bb=bi, w=mw):
                            def f(raw):
                                return raw.matmul(pbanks[b][0:w, :],
                                                  wkvcs[bb][:, kk * 640 + mm * 128: kk * 640 + mm * 128 + w],
                                                  xsl[bb][:, kk * S + nn * 512: kk * S + (nn + 1) * 512],
                                                  start=(kk == 0), stop=(kk == A_CH - 1))
                            return f
                        tk = em("te", mk(), waits=ws if k == 0 else (),
                                inc="te" if k == A_CH - 1 else None)
                    dstc = kvc[0:mw, 2048 + m * S + n * 512: 2048 + m * S + (n + 1) * 512]
                    if p == 0:
                        fn = (lambda ds, b, w=mw: lambda raw: raw.tensor_copy(ds, pbanks[b][0:w, :]))(dstc, bk)
                    else:
                        fn = (lambda ds, b, w=mw: lambda raw: raw.tensor_tensor(ds, pbanks[b][0:w, :], ds, AluOpType.add))(dstc, bk)
                    bank_free[bk] = em("ve", fn, waits=[tk], inc="ve")
            free_tok = tk
            xsl_free[bi] = free_tok
            xso_free[bi] = free_tok
            wqc_free[bi] = free_tok
            wkvc_free[bi] = free_tok
            if p + 1 < A_PASSES:
                phaseA_loads(p + 1)

        qc_ready = bank_free[(bank_i[0] - 1) % 4]
        kvc_ready = qc_ready


        ctxA.close()
        ctxB = ExitStack()
        EB = ctxB.enter_context
        wstg = EB(nc.sbuf_tensor("wstg", [128, 4 * 384], F32))
        wquph = EB(nc.sbuf_tensor("wquph", [128, 12 * 384], BF16))
        wqupl = EB(nc.sbuf_tensor("wqupl", [128, 12 * 384], BF16))
        wkvuph = EB(nc.sbuf_tensor("wkvuph", [128, 4 * 512], BF16))
        wkvupl = EB(nc.sbuf_tensor("wkvupl", [128, 4 * 512], BF16))
        qph = EB(nc.sbuf_tensor("qph", [128, 3 * OWN], BF16))
        qpl = EB(nc.sbuf_tensor("qpl", [128, 3 * OWN], BF16))
        knh = EB(nc.sbuf_tensor("knh", [128, S], BF16))
        knl = EB(nc.sbuf_tensor("knl", [128, S], BF16))
        vt = EB(nc.sbuf_tensor("vt", [128, 16 * VD], F32))
        Pex = EB(nc.sbuf_tensor("Pex", [128, S], F32))
        PT = EB(nc.sbuf_tensor("PT", [128, 16 * 256], F32))
        sums = EB(nc.sbuf_tensor("sums", [128, 8], F32))
        rrow = EB(nc.sbuf_tensor("rrow", [1, OWN], F32))
        rbs = EB(nc.sbuf_tensor("rbs", [128, 256], F32))
        negm = EB(nc.sbuf_tensor("negm", [128, 1], F32))
        rmax = EB(nc.sbuf_tensor("rmax", [128, 1], F32))
        wqup_free = [qc_ready, qc_ready]
        wkvup_free = [None, None]
        tok_wqup = [None, None]
        tok_wkvup = [None, None]

        wstg_free = [None]

        def phaseB_loads(hp):
            bi = 0
            c0 = hp * 2 * QH
            t_l1 = wstg_free[0]
            for hf in range(3):
                def msw(h6=hf, cc=c0):
                    src = d["wqupT"][512 * h6:512 * (h6 + 1), cc:cc + 384]
                    return dma3(wstg, src, 4, 384)
                t_s1 = em("sy", msw(), waits=[t_l1, wqup_free[bi]], inc="lqup", amt=16)
                def mswh(h6=hf):
                    def f(raw):
                        return raw.tensor_copy(wquph[:, 1536 * h6:1536 * (h6 + 1)], wstg[:])
                    return f
                t_h1 = em("ve", mswh(), waits=[t_s1, wqup_free[bi]], inc="ve")
                def mswl(h6=hf):
                    def f(raw):
                        return raw.tensor_tensor(wqupl[:, 1536 * h6:1536 * (h6 + 1)], wstg[:],
                                                 wquph[:, 1536 * h6:1536 * (h6 + 1)], AluOpType.subtract)
                    return f
                t_l1 = em("ve", mswl(), waits=[t_h1], inc="ve")
            tok_wqup[bi] = t_l1
            c1 = hp * 2 * (DN + VD)
            for hf in range(2):
                def msw2(h6=hf, cc=c1):
                    src = d["wkvupT"][256 * h6:256 * (h6 + 1), cc:cc + 512]
                    return dma3_into(wstg, 0, src, 2, 512)
                t_s2 = em("sy", msw2(), waits=[t_l1, wkvup_free[bi]], inc="lkvup", amt=16)
                def mswh2(h6=hf):
                    def f(raw):
                        return raw.tensor_copy(wkvuph[:, 1024 * h6:1024 * (h6 + 1)], wstg[:, 0:1024])
                    return f
                t_h2 = em("ve", mswh2(), waits=[t_s2, wkvup_free[bi]], inc="ve")
                def mswl2(h6=hf):
                    def f(raw):
                        return raw.tensor_tensor(wkvupl[:, 1024 * h6:1024 * (h6 + 1)], wstg[:, 0:1024],
                                                 wkvuph[:, 1024 * h6:1024 * (h6 + 1)], AluOpType.subtract)
                    return f
                t_l1 = em("ve", mswl2(), waits=[t_h2], inc="ve")
            tok_wkvup[bi] = t_l1
            wstg_free[0] = t_l1

        oacc_tok = em("ve", lambda raw: raw.memset(oacc[:], 0.0), inc="ve")

        qcb = qc.ap().bitcast(BF16)
        kvb = kvc.ap().bitcast(BF16)
        split_tok = qc_ready
        for src_t, bview, nsub in ((qc, qcb, 24), (kvc, kvb, 20)):
            lo_toks = [None] * nsub
            for c2 in range(nsub):
                prev = lo_toks[c2 - 4] if c2 >= 4 else None
                def mhi(t=src_t, bv=bview, c3=c2):
                    def f(raw):
                        return raw.tensor_copy(bv[:, c3 * 1024:c3 * 1024 + 512],
                                               t[:, 2048 + c3 * 512:2048 + (c3 + 1) * 512])
                    return f
                t_hi = em("ve", mhi(), waits=[qc_ready, prev], inc="ve")
                def mlo(t=src_t, bv=bview, c3=c2):
                    def f(raw):
                        return raw.tensor_tensor(bv[:, c3 * 1024 + 512:(c3 + 1) * 1024],
                                                 t[:, 2048 + c3 * 512:2048 + (c3 + 1) * 512],
                                                 bv[:, c3 * 1024:c3 * 1024 + 512],
                                                 AluOpType.subtract)
                    return f
                lo_toks[c2] = em("ve", mlo(), waits=[t_hi], inc="ve")
                split_tok = lo_toks[c2]
        qcs_ready = split_tok
        kvcs_ready = split_tok
        def qc_h(k, n):
            c3 = 2 * k + n
            return qcb[:, c3 * 1024:c3 * 1024 + 512]
        def qc_l(k, n):
            c3 = 2 * k + n
            return qcb[:, c3 * 1024 + 512:(c3 + 1) * 1024]
        def kv_h(m, n, p0=0, p1=128, w=512):
            c3 = 4 * m + n
            return kvb[p0:p1, c3 * 1024:c3 * 1024 + w]
        def kv_l(m, n, p0=0, p1=128, w=512):
            c3 = 4 * m + n
            return kvb[p0:p1, c3 * 1024 + 512:c3 * 1024 + 512 + w]

        phaseB_loads(0)
        qp_free = None
        kn_free = None
        vt_free = None
        pA_free = None
        pB_free = None
        pC_free = None
        pD_free = None
        pbig_free = qc_ready
        rbs_free = None
        rrow_free = None
        t_exp = None

        for hp in range(16):
            bi = 0
            qp_toks = []
            for m in range(3):
                for n in range(2):
                    bk = next_bank()
                    ws = [tok_wqup[bi], qcs_ready, bank_free[bk], qp_free]
                    tk = None
                    nmm = 0
                    for k in range(12):
                        for wv, xv in ((wquph, qc_h(k, n)), (wquph, qc_l(k, n)), (wqupl, qc_h(k, n))):
                            def mk(mm=m, kk=k, b=bk, w=wv, x=xv, first=(nmm == 0), last=(nmm == 35)):
                                def f(raw):
                                    return raw.matmul(pbanks[b][:],
                                                      w[:, kk * 384 + mm * 128: kk * 384 + (mm + 1) * 128],
                                                      x, start=first, stop=last)
                                return f
                            tk = em("te", mk(), waits=ws if nmm == 0 else (),
                                    inc="te" if nmm == 35 else None)
                            nmm += 1
                    dh = qph[:, m * OWN + n * 512: m * OWN + (n + 1) * 512]
                    dl = qpl[:, m * OWN + n * 512: m * OWN + (n + 1) * 512]
                    t_eh = em("ve", (lambda ds, b: lambda raw: raw.tensor_copy(ds, pbanks[b][:]))(dh, bk),
                              waits=[tk], inc="ve")
                    bank_free[bk] = em("ve",
                                       (lambda ds, b, hh3=dh: lambda raw: raw.tensor_tensor(ds, pbanks[b][:], hh3, AluOpType.subtract))(dl, bk),
                                       waits=[t_eh], inc="ve")
                    qp_toks.append(bank_free[bk])
            qp_ready = qp_toks[-1]
            qp_free = None

            for hh in range(2):
                kn_toks = []
                for n in range(4):
                    bk = next_bank()
                    ws = [tok_wkvup[bi], kvcs_ready, bank_free[bk], kn_free]
                    tk = None
                    nmm = 0
                    for k in range(4):
                        for wv, xv in ((wkvuph, kv_h(k, n)), (wkvuph, kv_l(k, n)), (wkvupl, kv_h(k, n))):
                            def mk(kk=k, b=bk, hh2=hh, w=wv, x=xv, first=(nmm == 0), last=(nmm == 11)):
                                def f(raw):
                                    return raw.matmul(pbanks[b][:],
                                                      w[:, kk * 512 + hh2 * 256: kk * 512 + hh2 * 256 + 128],
                                                      x, start=first, stop=last)
                                return f
                            tk = em("te", mk(), waits=ws if nmm == 0 else (),
                                    inc="te" if nmm == 11 else None)
                            nmm += 1
                    dh = knh[:, n * 512:(n + 1) * 512]
                    dl = knl[:, n * 512:(n + 1) * 512]
                    t_eh = em("ve", (lambda ds, b: lambda raw: raw.tensor_copy(ds, pbanks[b][:]))(dh, bk),
                              waits=[tk], inc="ve")
                    bank_free[bk] = em("ve",
                                       (lambda ds, b, hh3=dh: lambda raw: raw.tensor_tensor(ds, pbanks[b][:], hh3, AluOpType.subtract))(dl, bk),
                                       waits=[t_eh], inc="ve")
                    kn_toks.append(bank_free[bk])
                kn_ready = kn_toks[-1]
                kn_free = None

                vt_toks = []
                for tt in range(16):
                    bk = next_bank()
                    ws = [tok_wkvup[bi], kvcs_ready, bank_free[bk], vt_free]
                    tk = None
                    nmm = 0
                    for k in range(4):
                        kvh_s = kv_h(k, tt // 4, 0, 128, 512)
                        kvl_s = kv_l(k, tt // 4, 0, 128, 512)
                        off = (tt % 4) * 128
                        for lv, wv in ((kvh_s, wkvuph), (kvl_s, wkvuph), (kvh_s, wkvupl)):
                            def mk(kk=k, b=bk, hh2=hh, lh=lv, w=wv, o=off, first=(nmm == 0), last=(nmm == 11)):
                                def f(raw):
                                    return raw.matmul(pbanks[b][:, 0:128],
                                                      lh[:, o:o + 128],
                                                      w[:, kk * 512 + hh2 * 256 + 128: kk * 512 + hh2 * 256 + 256],
                                                      start=first, stop=last)
                                return f
                            tk = em("te", mk(), waits=ws if nmm == 0 else (),
                                    inc="te" if nmm == 11 else None)
                            nmm += 1
                    dstc = vt[:, tt * VD:(tt + 1) * VD]
                    bank_free[bk] = em("ve",
                                       (lambda ds, b: lambda raw: raw.tensor_copy(ds, pbanks[b][:, 0:128]))(dstc, bk),
                                       waits=[tk], inc="ve")
                    vt_toks.append(bank_free[bk])
                vt_ready = vt_toks[-1]
                vt_free = None

                def q_nope_hl(qt, hh2=hh):
                    a, b2 = hh2 * OWN + qt * 128, hh2 * OWN + (qt + 1) * 128
                    return qph[:, a:b2], qpl[:, a:b2]

                def q_rope_hl(qt, hh2=hh):
                    a, b2 = 2 * OWN + qt * 128, 2 * OWN + (qt + 1) * 128
                    if hh2 == 0:
                        return qph[0:64, a:b2], qpl[0:64, a:b2]
                    return qph[64:128, a:b2], qpl[64:128, a:b2]

                rope_lo = (hh == 0)

                for qg in range(4):
                    for qi in range(2):
                        qt = qg * 2 + qi
                        qah, qal = q_nope_hl(qt)
                        qbh, qbl = q_rope_hl(qt)
                        p0, p1 = (0, 64) if rope_lo else (64, 128)
                        sc_tok = None
                        ws0 = [qp_ready, kn_ready, vt_ready, pbig_free]
                        for kt in range(4):
                            ops = [(qah, knh[:, kt * 512:(kt + 1) * 512]),
                                   (qal, knh[:, kt * 512:(kt + 1) * 512]),
                                   (qah, knl[:, kt * 512:(kt + 1) * 512]),
                                   (qbh, kv_h(4, kt, p0, p1)),
                                   (qbl, kv_h(4, kt, p0, p1)),
                                   (qbh, kv_l(4, kt, p0, p1))]
                            for i2, (lh, rh) in enumerate(ops):
                                def mk1(k2=kt, a=lh, b2=rh, first=(i2 == 0), last=(i2 == 5)):
                                    def f(raw):
                                        return raw.matmul(pbig[:, k2 * 512:(k2 + 1) * 512],
                                                          a, b2, start=first, stop=last)
                                    return f
                                sc_tok = em("te", mk1(), waits=ws0 if (kt == 0 and i2 == 0) else (),
                                            inc="te" if i2 == 5 else None)
                        t_mx = em("ve", lambda raw: raw.reduce_max(rmax[:], pbig[:], axis=AX),
                                  waits=[sc_tok], inc="ve")
                        t_ng = em("ve", lambda raw: raw.tensor_scalar(negm[:], rmax[:], -INV_SQRT_QH, None, AluOpType.mult),
                                  waits=[t_mx], inc="ve")
                        def mkexp(q2=qt):
                            def f(raw):
                                return raw.activation(Pex[:], pbig[:], AF.Exp, bias=negm[:],
                                                      scale=INV_SQRT_QH, accum_out=sums[:, q2:q2 + 1])
                            return f
                        t_exp = em("se", mkexp(), waits=[t_ng], inc="se")
                        pbig_free = t_exp
                        for g4 in range(4):
                            even = (g4 % 2 == 0)
                            pbank = pA if even else pB
                            pfree = pA_free if even else pB_free
                            ttr = None
                            for j in range(4):
                                kt2 = g4 * 4 + j
                                def mtr(k3=kt2, jj=j, pb=pbank):
                                    def f(raw):
                                        return raw.matmul(pb[:, jj * 128:(jj + 1) * 128],
                                                          Pex[:, k3 * 128:(k3 + 1) * 128], idt[:],
                                                          start=(jj == 0), stop=(jj == 3),
                                                          is_transpose=True)
                                    return f
                                ttr = em("te", mtr(), waits=[t_exp, pfree, t_const] if j == 0 else (),
                                         inc="te" if j == 3 else None)
                            def mev(g5=g4, q2=qi, pb=pbank):
                                def f(raw):
                                    dst = PT.ap().rearrange("p (k q c) -> p k q c", q=2, c=128)[:, g5 * 4:(g5 + 1) * 4, q2, :]
                                    return raw.copy(dst, pb[:])
                                return f
                            tev = em("se", mev(), waits=[ttr], inc="se")
                            if even:
                                pA_free = tev
                            else:
                                pB_free = tev
                    pv_ws = [pA_free, pB_free, vt_ready, pC_free]
                    t_pv = None
                    for kt2 in range(16):
                        def mpv(k3=kt2):
                            def f(raw):
                                return raw.matmul(pC[:, 0:256],
                                                  vt[:, k3 * VD:(k3 + 1) * VD],
                                                  PT[:, k3 * 256:(k3 + 1) * 256],
                                                  start=(k3 == 0), stop=(k3 == 15))
                            return f
                        t_pv = em("te", mpv(), waits=pv_ws if kt2 == 0 else (),
                                  inc="te" if kt2 == 15 else None)
                    t_tr2 = None
                    for qi2 in range(2):
                        def mtr2(q3=qg * 2 + qi2, j2=qi2):
                            def f(raw):
                                return raw.matmul(pD[0:1, j2 * 128:(j2 + 1) * 128],
                                                  sums[:, q3:q3 + 1], idt[:],
                                                  start=(j2 == 0), stop=(j2 == 1),
                                                  is_transpose=True)
                            return f
                        t_tr2 = em("te", mtr2(), waits=[t_exp, t_const, (pD_free if qi2 == 0 else None)],
                                   inc="te" if qi2 == 1 else None)
                    def mrc(q2=qg):
                        def f(raw):
                            return raw.reciprocal(rrow[0:1, q2 * 256:(q2 + 1) * 256], pD[0:1, 0:256])
                        return f
                    t_rc = em("ve", mrc(), waits=[t_tr2, rrow_free], inc="ve")
                    def mbc(q2=qg):
                        def f(raw):
                            return raw.matmul(pD[:, 256:512], on1[:], rrow[0:1, q2 * 256:(q2 + 1) * 256],
                                              start=True, stop=True)
                        return f
                    t_bc = em("te", mbc(), waits=[t_rc, t_const], inc="te")
                    t_rb = em("se", lambda raw: raw.copy(rbs[:], pD[:, 256:512]),
                              waits=[t_bc, rbs_free], inc="se")
                    pD_free = t_rb
                    t_sc2 = em("ve", lambda raw: raw.tensor_tensor(rbs[:], pC[:, 0:256], rbs[:], AluOpType.mult),
                               waits=[t_pv, t_rb, oacc_tok], inc="ve")
                    pC_free = t_sc2
                    rbs_free = t_sc2
                    def macc(q2=qg):
                        def f(raw):
                            ds = oacc[:, q2 * 256:(q2 + 1) * 256]
                            return raw.tensor_tensor(ds, rbs[:], ds, AluOpType.add)
                        return f
                    oacc_tok = em("ve", macc(), waits=[t_sc2], inc="ve")
                    rrow_free = t_bc
                kn_free = sc_tok
                vt_free = t_pv
            qp_free = sc_tok
            wqup_free[bi] = qp_ready
            wkvup_free[bi] = vt_ready
            if hp + 1 < 16:
                phaseB_loads(hp + 1)

        ctxB.close()
        ctxAB.close()
        ctxC = ctx.enter_context(ExitStack())
        EC = ctxC.enter_context
        x2 = EC(nc.sbuf_tensor("x2", [128, 32 * CT], F32))
        yt_ = EC(nc.sbuf_tensor("yt_", [128, 32 * CT], F32))
        wos = EC(nc.sbuf_tensor("wos", [128, H], F32))
        wbs = EC(nc.sbuf_tensor("wbs", [64, H], F32))
        wstgC = EC(nc.sbuf_tensor("wstgC", [128, 32 * 128], F32))
        wstgC2 = EC(nc.sbuf_tensor("wstgC2", [128, 16 * 128], F32))
        wash = EC(nc.sbuf_tensor("wash", [128, 32 * 64], BF16))
        wasl = EC(nc.sbuf_tensor("wasl", [128, 32 * 64], BF16))
        wgsh = EC(nc.sbuf_tensor("wgsh", [128, 32 * 128], BF16))
        wgsl = EC(nc.sbuf_tensor("wgsl", [128, 32 * 128], BF16))
        wfh = [EC(nc.sbuf_tensor(f"wfh{i}", [128, 32 * 128], BF16)) for i in range(2)]
        wfl = [EC(nc.sbuf_tensor(f"wfl{i}", [128, 32 * 128], BF16)) for i in range(2)]
        scrC = EC(nc.sbuf_tensor("scrC", [128, CT], F32))
        xoc = [EC(nc.sbuf_tensor(f"xoc{i}", [128, CT], F32)) for i in range(2)]
        sqs = EC(nc.sbuf_tensor("sqs", [128, CT], F32))
        us = EC(nc.sbuf_tensor("us", [64, CT], F32))
        ums = EC(nc.sbuf_tensor("ums", [64, CT], F32))
        srow = EC(nc.sbuf_tensor("srow", [1, 4 * CT], F32))
        arow = EC(nc.sbuf_tensor("arow", [1, 4 * CT], F32))
        lrow = EC(nc.sbuf_tensor("lrow", [1, 4 * CT], F32))
        mub = EC(nc.sbuf_tensor("mub", [128, CT], F32))
        isb = EC(nc.sbuf_tensor("isb", [128, CT], F32))
        em("sy", dma(wos[:], d["woT"]), waits=[oacc_tok], inc="ldc", amt=16)
        em("sy", dma(wbs[:], d["wbT"]), waits=[oacc_tok], inc="ldc", amt=16)
        t_sa = em("sy", dma3_into(wstgC, 0, d["waT"], 32, 64), waits=[oacc_tok], inc="ldc", amt=16)
        t_ah = em("ve", lambda raw: raw.tensor_copy(wash[:], wstgC[:, 0:2048]), waits=[t_sa], inc="ve")
        t_al = em("ve", lambda raw: raw.tensor_tensor(wasl[:], wstgC[:, 0:2048], wash[:], AluOpType.subtract),
                  waits=[t_ah], inc="ve")
        t_sg = em("sy", dma3(wstgC, d["wgP"], 32, 128), waits=[t_al], inc="ldc", amt=16)
        t_gh = em("ve", lambda raw: raw.tensor_copy(wgsh[:], wstgC[:]), waits=[t_sg], inc="ve")
        t_gl = em("ve", lambda raw: raw.tensor_tensor(wgsl[:], wstgC[:], wgsh[:], AluOpType.subtract),
                  waits=[t_gh], inc="ve")
        t_const = ("ldc", 64)
        t_wsplit = t_gl
        wfstg_free = [t_gl, t_gl]
        wfs_free = [oacc_tok, oacc_tok]
        tok_wfs = [None, None]
        xoc_free = [oacc_tok, oacc_tok]
        tok_xoc = [None, None]
        x2_free = None
        yt_free = None
        pr_free = {"pA": pA_free, "pB": pB_free, "pC": pC_free, "pD": pD_free}
        pbig_free_l = [oacc_tok] * 4
        mub_free = None
        isb_free = None
        lrow_free = None
        arow_free = None
        srow_free = None
        ums_free = None
        us_free = None
        sqs_free = None

        L = lambda i: lrow[0:1, i * CT:(i + 1) * CT]
        SR = lambda g: srow[0:1, g * CT:(g + 1) * CT]
        AR = lambda i: arow[0:1, i * CT:(i + 1) * CT]

        def ln_rows(s1_tok, s2_tok):
            t_mu = em("ve", lambda raw: raw.tensor_scalar(L(0), pC[0:1, 0:CT], 1.0 / H, None, AluOpType.mult),
                      waits=[s1_tok, lrow_free], inc="ve")
            t_ms = em("ve", lambda raw: raw.tensor_scalar(L(1), pD[0:1, 0:CT], 1.0 / H, None, AluOpType.mult),
                      waits=[s2_tok], inc="ve")
            t_m2 = em("ve", lambda raw: raw.tensor_tensor(L(2), L(0), L(0), AluOpType.mult),
                      waits=[t_mu], inc="ve")
            t_va = em("ve", lambda raw: raw.tensor_tensor(L(2), L(1), L(2), AluOpType.subtract),
                      waits=[t_ms, t_m2], inc="ve")
            t_ve2 = em("ve", lambda raw: raw.tensor_scalar(L(2), L(2), 1e-5, None, AluOpType.add),
                       waits=[t_va], inc="ve")
            t_ln = em("se", lambda raw: raw.activation(L(3), L(2), AF.Ln), waits=[t_ve2], inc="se")
            t_is = em("se", lambda raw: raw.activation(L(3), L(3), AF.Exp, scale=-0.5), waits=[t_ln], inc="se")
            pr_free["pC"] = t_mu
            pr_free["pD"] = t_ms
            t_b1 = em("te", lambda raw: raw.matmul(pA[:, 0:CT], on1[:], L(0), start=True, stop=True),
                      waits=[t_mu, t_const, pr_free["pA"]], inc="te")
            t_c1 = em("se", lambda raw: raw.copy(mub[:], pA[:, 0:CT]), waits=[t_b1, mub_free], inc="se")
            pr_free["pA"] = t_c1
            t_b2 = em("te", lambda raw: raw.matmul(pB[:, 0:CT], on1[:], L(3), start=True, stop=True),
                      waits=[t_is, t_const, pr_free["pB"]], inc="te")
            t_c2 = em("se", lambda raw: raw.copy(isb[:], pB[:, 0:CT]), waits=[t_b2, isb_free], inc="se")
            pr_free["pB"] = t_c2
            return t_c1, t_c2, t_b2

        for ct in range(NCT):
            tcol = ct * CT
            s1_tok = None
            s2_tok = None
            x2_toks = []
            for m in range(32):
                bk = m % 2
                def mwo(mm=m, b=bk, c0=tcol):
                    def f(raw):
                        return raw.matmul(pbanks[b][:, 0:CT], wos[:, mm * 128:(mm + 1) * 128],
                                          oacc[:, c0:c0 + CT], start=True, stop=True)
                    return f
                t_wo = em("te", mwo(), waits=[oacc_tok, t_const, pbig_free_l[bk]], inc="te")
                xb = m % 2
                def mxo(mm=m, b2=xb, c0=tcol):
                    def f(raw):
                        return raw.dma_start(out=xoc[b2][:], in_=d["xTown"][mm * 128:(mm + 1) * 128, c0:c0 + CT])
                    return f
                tok_xoc[xb] = em("sy", mxo(), waits=[xoc_free[xb]], inc=("lxo1" if xb else "lxo0"), amt=16)
                def mev2(mm=m, b=bk, b2=xb):
                    def f(raw):
                        return raw.tensor_tensor(x2[:, mm * CT:(mm + 1) * CT], pbanks[b][:, 0:CT],
                                                 xoc[b2][:], AluOpType.add)
                    return f
                t_x2 = em("ve", mev2(), waits=[t_wo, tok_xoc[xb], (x2_free if m == 0 else None)], inc="ve")
                x2_toks.append(t_x2)
                xoc_free[xb] = t_x2
                pbig_free_l[bk] = t_x2
                def ms1(mm=m):
                    def f(raw):
                        return raw.matmul(pC[0:1, 0:CT], onc[:], x2[:, mm * CT:(mm + 1) * CT],
                                          start=(mm == 0), stop=(mm == 31))
                    return f
                s1_tok = em("te", ms1(), waits=[t_x2, t_const, (pr_free["pC"] if m == 0 else None)],
                            inc="te" if m == 31 else None)
                def msq(mm=m):
                    def f(raw):
                        return raw.square(sqs[:], x2[:, mm * CT:(mm + 1) * CT])
                    return f
                t_sq = em("se", msq(), waits=[t_x2, sqs_free], inc="se")
                def ms2(mm=m):
                    def f(raw):
                        return raw.matmul(pD[0:1, 0:CT], onc[:], sqs[:],
                                          start=(mm == 0), stop=(mm == 31))
                    return f
                s2_tok = em("te", ms2(), waits=[t_sq, (pr_free["pD"] if m == 0 else None)],
                            inc="te" if m == 31 else None)
                sqs_free = s2_tok
            t_c1, t_c2, _ = ln_rows(s1_tok, s2_tok)
            lrow_free = pr_free["pB"]
            x2n_toks = []
            for m in range(32):
                def mn1(mm=m):
                    def f(raw):
                        ds = x2[:, mm * CT:(mm + 1) * CT]
                        return raw.tensor_tensor(ds, ds, mub[:], AluOpType.subtract)
                    return f
                t_n1 = em("ve", mn1(), waits=[t_c1, x2_toks[m]], inc="ve")
                def mn2(mm=m):
                    def f(raw):
                        ds = x2[:, mm * CT:(mm + 1) * CT]
                        return raw.tensor_tensor(ds, ds, isb[:], AluOpType.mult)
                    return f
                t_n2 = em("ve", mn2(), waits=[t_c2, t_n1], inc="ve")
                x2n_toks.append(t_n2)
            mub_free = x2n_toks[-1]
            isb_free = x2n_toks[-1]
            x2b = x2.ap().bitcast(BF16)
            scrC_free = None
            sp_tok = None
            for m in range(32):
                def mcp2(mm=m):
                    def f(raw):
                        return raw.tensor_copy(scrC[:], x2[:, mm * CT:(mm + 1) * CT])
                    return f
                t_cp2 = em("ve", mcp2(), waits=[x2n_toks[m], scrC_free], inc="ve")
                def mhi2(mm=m):
                    def f(raw):
                        return raw.tensor_copy(x2b[:, mm * 2 * CT:mm * 2 * CT + CT], scrC[:])
                    return f
                t_hi2 = em("ve", mhi2(), waits=[t_cp2], inc="ve")
                def mlo2(mm=m):
                    def f(raw):
                        return raw.tensor_tensor(x2b[:, mm * 2 * CT + CT:(mm + 1) * 2 * CT],
                                                 scrC[:], x2b[:, mm * 2 * CT:mm * 2 * CT + CT],
                                                 AluOpType.subtract)
                    return f
                sp_tok = em("ve", mlo2(), waits=[t_hi2], inc="ve")
                scrC_free = sp_tok
            x2_ready = sp_tok
            def x2_h(k):
                return x2b[:, k * 2 * CT:k * 2 * CT + CT]
            def x2_l(k):
                return x2b[:, k * 2 * CT + CT:(k + 1) * 2 * CT]

            g_tok = None
            nmm = 0
            for k in range(32):
                for wv, xv in ((wgsh, x2_h(k)), (wgsh, x2_l(k)), (wgsl, x2_h(k))):
                    def mg(kk=k, w=wv, x=xv, first=(nmm == 0), last=(nmm == 95)):
                        def f(raw):
                            return raw.matmul(pC[:, 0:CT], w[:, kk * 128:(kk + 1) * 128],
                                              x, start=first, stop=last)
                        return f
                    g_tok = em("te", mg(), waits=([x2_ready, t_const, t_wsplit, pr_free["pC"]] if nmm == 0 else ()),
                               inc="te" if nmm == 95 else None)
                    nmm += 1
            s_cp = []
            for g in range(4):
                def msg(gg=g):
                    def f(raw):
                        return raw.copy(srow[0:1, gg * CT:(gg + 1) * CT], pC[32 * gg:32 * gg + 1, 0:CT])
                    return f
                s_cp.append(em("se", msg(), waits=[g_tok, srow_free] if g == 0 else [g_tok], inc="se"))
            s_ad = []
            for g in range(4):
                def msa(gg=g):
                    def f(raw):
                        return raw.tensor_scalar(SR(gg), SR(gg), gbs[0:1, gg:gg + 1], None, AluOpType.add)
                    return f
                s_ad.append(em("ve", msa(), waits=[s_cp[g]], inc="ve"))
            pr_free["pC"] = s_cp[-1]
            s_tok = s_ad[-1]
            t1 = em("ve", lambda raw: raw.tensor_tensor(AR(0), SR(2), SR(3), AluOpType.max),
                    waits=[s_tok, arow_free], inc="ve")
            t2 = em("ve", lambda raw: raw.tensor_tensor(AR(1), SR(1), AR(0), AluOpType.max),
                    waits=[t1], inc="ve")
            t3 = em("ve", lambda raw: raw.tensor_tensor(AR(1), AR(1), SR(0), AluOpType.is_gt),
                    waits=[t2], inc="ve")
            t4 = em("ve", lambda raw: raw.tensor_tensor(AR(2), SR(0), SR(1), AluOpType.max),
                    waits=[t3], inc="ve")
            t5 = em("ve", lambda raw: raw.tensor_tensor(AR(2), AR(0), AR(2), AluOpType.is_gt),
                    waits=[t4], inc="ve")
            t6 = em("ve", lambda raw: raw.tensor_tensor(AR(0), SR(0), SR(1), AluOpType.max),
                    waits=[t5], inc="ve")
            t7 = em("ve", lambda raw: raw.tensor_tensor(AR(0), AR(0), SR(2), AluOpType.max),
                    waits=[t6], inc="ve")
            t8 = em("ve", lambda raw: raw.tensor_tensor(AR(0), SR(3), AR(0), AluOpType.is_gt),
                    waits=[t7], inc="ve")
            srow_free = t8
            em("te", lambda raw: raw.matmul(pB[:, 0:CT], sgp[0:1, 0:128], onr[0:1, 0:CT], start=True, stop=False),
               waits=[t8, t_const, pr_free["pB"]])
            em("te", lambda raw: raw.matmul(pB[:, 0:CT], sgp[0:1, 128:256], AR(1), start=False, stop=False))
            em("te", lambda raw: raw.matmul(pB[:, 0:CT], sgp[0:1, 256:384], AR(2), start=False, stop=False))
            m_tok = em("te", lambda raw: raw.matmul(pB[:, 0:CT], sgp[0:1, 384:512], AR(0), start=False, stop=True),
                       inc="te")
            arow_free = m_tok
            u_tok = None
            nmm = 0
            for k in range(32):
                for wv, xv in ((wash, x2_h(k)), (wash, x2_l(k)), (wasl, x2_h(k))):
                    def mu(kk=k, w=wv, x=xv, first=(nmm == 0), last=(nmm == 95)):
                        def f(raw):
                            return raw.matmul(pA[0:64, 0:CT], w[:, kk * 64:(kk + 1) * 64],
                                              x, start=first, stop=last)
                        return f
                    u_tok = em("te", mu(), waits=([x2_ready, t_const, t_wsplit, pr_free["pA"]] if nmm == 0 else ()),
                               inc="te" if nmm == 95 else None)
                    nmm += 1
            t_ue = em("se", lambda raw: raw.copy(us[:], pA[0:64, 0:CT]), waits=[u_tok, us_free], inc="se")
            pr_free["pA"] = t_ue
            t_um = em("ve", lambda raw: raw.tensor_tensor(ums[:], us[:], pB[0:64, 0:CT], AluOpType.mult),
                      waits=[t_ue, m_tok, ums_free], inc="ve")
            us_free = t_um
            pr_free["pB"] = t_um

            s1_tok = None
            s2_tok = None
            y_toks = []
            t_d = None
            def wf_fetch(mm):
                wb2 = mm % 2
                t_prev = wfstg_free[0]
                for hf2 in range(2):
                    def msg2(mm2=mm, h7=hf2):
                        src = d["wffnT"][2048 * h7:2048 * (h7 + 1), mm2 * 128:(mm2 + 1) * 128]
                        return dma3(wstgC2, src, 16, 128)
                    t_sg = em("sy", msg2(), waits=[t_prev, t_wsplit], inc=("lwf1" if wb2 else "lwf0"), amt=16)
                    def msh2(b3=wb2, h7=hf2):
                        def f(raw):
                            return raw.tensor_copy(wfh[b3][:, 2048 * h7:2048 * (h7 + 1)], wstgC2[:])
                        return f
                    t_h = em("ve", msh2(), waits=[t_sg, wfs_free[wb2]], inc="ve")
                    def msl2(b3=wb2, h7=hf2):
                        def f(raw):
                            return raw.tensor_tensor(wfl[b3][:, 2048 * h7:2048 * (h7 + 1)], wstgC2[:],
                                                     wfh[b3][:, 2048 * h7:2048 * (h7 + 1)], AluOpType.subtract)
                        return f
                    t_prev = em("ve", msl2(), waits=[t_h], inc="ve")
                wfstg_free[0] = t_prev
                tok_wfs[wb2] = t_prev

            if ct == 0:
                wf_fetch(0)
            for m in range(32):
                wb = m % 2
                if m + 1 < 32:
                    wf_fetch(m + 1)
                elif ct + 1 < NCT:
                    wf_fetch(0)
                bk = 2 + (m % 2)
                nmm = 0
                for k in range(32):
                    for wv, xv in ((wfh[wb], x2_h(k)), (wfh[wb], x2_l(k)), (wfl[wb], x2_h(k))):
                        def md(kk=k, b=bk, w=wv, x=xv, first=(nmm == 0)):
                            def f(raw):
                                return raw.matmul(pbanks[b][:, 0:CT], w[:, kk * 128:(kk + 1) * 128],
                                                  x, start=first, stop=False)
                            return f
                        em("te", md(), waits=[tok_wfs[wb], x2_ready, t_um, pbig_free_l[bk]] if nmm == 0 else ())
                        nmm += 1
                def mdb(mm=m, b=bk):
                    def f(raw):
                        return raw.matmul(pbanks[b][:, 0:CT], wbs[:, mm * 128:(mm + 1) * 128],
                                          ums[:], start=False, stop=True)
                    return f
                t_d = em("te", mdb(), inc="te")
                wfs_free[wb] = t_d
                def mye(mm=m, b=bk):
                    def f(raw):
                        return raw.scalar_tensor_tensor(yt_[:, mm * CT:(mm + 1) * CT],
                                                        pbanks[b][:, 0:CT], bfs[:, mm:mm + 1],
                                                        x2_h(mm), AluOpType.add, AluOpType.add)
                    return f
                t_y0 = em("ve", mye(), waits=[t_d, (yt_free if m == 0 else None)], inc="ve")
                def mye2(mm=m):
                    def f(raw):
                        ds = yt_[:, mm * CT:(mm + 1) * CT]
                        return raw.tensor_tensor(ds, ds, x2_l(mm), AluOpType.add)
                    return f
                t_y = em("ve", mye2(), waits=[t_y0], inc="ve")
                y_toks.append(t_y)
                pbig_free_l[bk] = t_y
                def ms1b(mm=m):
                    def f(raw):
                        return raw.matmul(pC[0:1, 0:CT], onc[:], yt_[:, mm * CT:(mm + 1) * CT],
                                          start=(mm == 0), stop=(mm == 31))
                    return f
                s1_tok = em("te", ms1b(), waits=[t_y, t_const, (pr_free["pC"] if m == 0 else None)],
                            inc="te" if m == 31 else None)
                def msqb(mm=m):
                    def f(raw):
                        return raw.square(sqs[:], yt_[:, mm * CT:(mm + 1) * CT])
                    return f
                t_sq = em("se", msqb(), waits=[t_y, sqs_free], inc="se")
                def ms2b(mm=m):
                    def f(raw):
                        return raw.matmul(pD[0:1, 0:CT], onc[:], sqs[:],
                                          start=(mm == 0), stop=(mm == 31))
                    return f
                s2_tok = em("te", ms2b(), waits=[t_sq, (pr_free["pD"] if m == 0 else None)],
                            inc="te" if m == 31 else None)
                sqs_free = s2_tok
            x2_free = y_toks[-1]
            ums_free = t_d
            t_c1, t_c2, _ = ln_rows(s1_tok, s2_tok)
            lrow_free = pr_free["pB"]
            out_toks = []
            for m in range(32):
                def mn1(mm=m):
                    def f(raw):
                        ds = yt_[:, mm * CT:(mm + 1) * CT]
                        return raw.tensor_tensor(ds, ds, mub[:], AluOpType.subtract)
                    return f
                t_n1 = em("ve", mn1(), waits=[t_c1, y_toks[m]], inc="ve")
                def mn2(mm=m):
                    def f(raw):
                        ds = yt_[:, mm * CT:(mm + 1) * CT]
                        return raw.tensor_tensor(ds, ds, isb[:], AluOpType.mult)
                    return f
                t_n2 = em("ve", mn2(), waits=[t_c2, t_n1], inc="ve")
                def mst(mm=m, c0=tcol):
                    def f(raw):
                        return raw.dma_start(out=yT[mm * 128:(mm + 1) * 128, c0:c0 + CT],
                                             in_=yt_[:, mm * CT:(mm + 1) * CT])
                    return f
                out_toks.append(em("gp", mst(), waits=[t_n2], inc="out", amt=16))
            yt_free = out_toks[-1]
            mub_free = out_toks[-1]
            isb_free = out_toks[-1]

        em("gp", lambda raw: raw.nop(), waits=[("out", P.cnt["out"])])

        @block.tensor
        def _(tensor):
            P.replay("te", tensor, sems)

        @block.vector
        def _(vector):
            P.replay("ve", vector, sems)

        @block.scalar
        def _(scalar):
            P.replay("se", scalar, sems)

        @block.sync
        def _(sync):
            P.replay("sy", sync, sems)

        @block.gpsimd
        def _(gpsimd):
            P.replay("gp", gpsimd, sems)

    return nc


def _prep_inputs(L0_attn_hidden_size, L0_attn_Wq_c, L0_attn_Wq_up, L0_attn_Wkv_c,
                 L0_attn_Wkv_up, L0_attn_Wo, L0_ffn_W, L0_ffn_b, L0_ffn_A, L0_ffn_B,
                 L0_ffn_gate_w, L0_ffn_gate_b):
    c = np.ascontiguousarray
    x = np.asarray(L0_attn_hidden_size, dtype=np.float32)
    wqcT = c(np.asarray(L0_attn_Wq_c, np.float32).T)
    wqup = np.asarray(L0_attn_Wq_up, np.float32)
    wq3 = wqup.reshape(NH, QH, QL)
    cols = []
    for hp in range(NH // 2):
        h0, h1 = 2 * hp, 2 * hp + 1
        cols.append(wq3[h0, :DN])
        cols.append(wq3[h1, :DN])
        cols.append(wq3[h0, DN:])
        cols.append(wq3[h1, DN:])
    wqupT = c(np.concatenate(cols, axis=0).T)
    wkvc = np.asarray(L0_attn_Wkv_c, np.float32)
    wkvc_p = np.zeros((640, H), np.float32)
    wkvc_p[:576] = wkvc
    wkvc_p[576:640] = wkvc[512:576]
    wkvcT = c(wkvc_p.T)
    wkvupT = c(np.asarray(L0_attn_Wkv_up, np.float32).T)
    woT = c(np.asarray(L0_attn_Wo, np.float32).T)
    wffnT = c(np.asarray(L0_ffn_W, np.float32).T)
    waT = c(np.asarray(L0_ffn_A, np.float32).T)
    wbT = c(np.asarray(L0_ffn_B, np.float32).T)
    wg = np.asarray(L0_ffn_gate_w, np.float32)
    wgP = np.zeros((H, 128), np.float32)
    for g in range(4):
        wgP[:, 32 * g] = wg[g]
    bffn = c(np.asarray(L0_ffn_b, np.float32).reshape(32, 128).T)
    gateb = c(np.asarray(L0_ffn_gate_b, np.float32).reshape(1, 4))
    ident = np.eye(128, dtype=np.float32)
    ones1 = np.ones((1, 128), np.float32)
    onesc = np.ones((128, 1), np.float32)
    onesr = np.ones((1, 512), np.float32)
    segp = np.zeros((1, 512), np.float32)
    segp[0, 0:8] = 1.0
    segp[0, 128 + 8:128 + 16] = 1.0
    segp[0, 256 + 16:256 + 32] = 1.0
    segp[0, 384 + 32:384 + 64] = 1.0
    shared = dict(wqcT=wqcT, wqupT=wqupT, wkvcT=wkvcT, wkvupT=wkvupT, woT=woT,
                  wffnT=wffnT, waT=waT, wbT=wbT, wgP=wgP, bffn=bffn, gateb=gateb,
                  ident=ident, ones1=ones1, onesc=onesc, onesr=onesr, segp=segp)
    in_maps = []
    for core in range(8):
        b = core // 2
        off = (core % 2) * OWN
        xT = c(x[b].T)
        m = dict(shared)
        m["xTfull"] = xT
        m["xTown"] = c(xT[:, off:off + OWN])
        in_maps.append(m)
    return in_maps


def kernel(L0_attn_hidden_size, L0_attn_Wq_c, L0_attn_bq_c, L0_attn_Wq_up, L0_attn_bq_up,
           L0_attn_Wkv_c, L0_attn_bkv_c, L0_attn_Wkv_up, L0_attn_bkv_up, L0_attn_Wo,
           L0_attn_bo, L0_ffn_W, L0_ffn_b, L0_ffn_A, L0_ffn_B, L0_ffn_gate_w, L0_ffn_gate_b):
    in_maps = _prep_inputs(L0_attn_hidden_size, L0_attn_Wq_c, L0_attn_Wq_up, L0_attn_Wkv_c,
                           L0_attn_Wkv_up, L0_attn_Wo, L0_ffn_W, L0_ffn_b, L0_ffn_A,
                           L0_ffn_B, L0_ffn_gate_w, L0_ffn_gate_b)
    nc = build_nc()
    res = run_bass_kernel_spmd(nc, in_maps, core_ids=list(range(8)))
    out = np.empty((B, S, H), dtype=np.float32)
    for core in range(8):
        b = core // 2
        off = (core % 2) * OWN
        out[b, off:off + OWN, :] = res.results[core]["yT"].T
    return out
